# version 10 (final)
# baseline (speedup 1.0000x reference)
import numpy as np
from contextlib import ExitStack

import concourse.bass as bass
import concourse.mybir as mybir
from concourse.bass_utils import run_bass_kernel_spmd

F32 = mybir.dt.float32
BF16 = mybir.dt.bfloat16
I32 = mybir.dt.int32
AF = mybir.ActivationFunctionType
ALU = mybir.AluOpType

L = 4096
DC = 8
D = 1024
H = 3072
HC = 24
NH = 16
HD = 64
TG = 512
NG = L // TG
NT = L // 128
QSCALE = 1.0 / 127.0
EPS_LN = 1e-5
EPS_ILN = 1e-12

ENGS = ("sy", "gp", "te", "ve", "se")
DMA_R = 4
SEM_NAMES = ["te", "ve", "se", "gpc"] + [f"sy{i}" for i in range(DMA_R)] + [
    f"gp{i}" for i in range(DMA_R)]

F32P_N = 35328
BF16P_N = 32768
AB_OFF = 34816
BWCW_OFF = 29696
P1ST_OFF = 23552
KVKS_OFF = 31728


class Pool:

    def __init__(self, tensor, ncols):
        self.t = tensor
        self.n = ncols
        self.off = 0
        self.marks = []

    def alloc(self, ncols):
        assert self.off + ncols <= self.n, f"pool overflow: {self.off}+{ncols}>{self.n}"
        ap = self.t[:, self.off:self.off + ncols]
        self.off += ncols
        return ap

    def push(self):
        self.marks.append(self.off)

    def pop(self):
        self.off = self.marks.pop()


class Sched:

    def __init__(self, nc, sems):
        self.nc = nc
        self.sems = sems
        self.cnt = {k: 0 for k in sems}
        self.q = {k: [] for k in ENGS}
        self.dma_idx = {"sy": 0, "gp": 0}

    def emit(self, eng, fn, waits=(), inc=None, inc_amt=1):
        waits = tuple((s, v) for s, v in waits if v > 0)
        for s, v in waits:
            assert v <= self.cnt[s], f"wait on future: {s} {v} > {self.cnt[s]}"
        if inc is not None:
            self.cnt[inc] += inc_amt
            post = self.cnt[inc]
            sem = self.sems[inc]
        else:
            post = None
            sem = None
        amt = inc_amt

        def thunk(e, fn=fn, waits=waits, sem=sem, amt=amt):
            for s, v in waits:
                e.wait_ge(self.sems[s], v)
            ins = fn(e)
            if sem is not None:
                assert ins is not None
                ins.then_inc(sem, amt)
        self.q[eng].append(thunk)
        return post

    def dma(self, eng, out_ap, in_ap, waits=()):
        i = self.dma_idx[eng]
        self.dma_idx[eng] = i + 1
        sem = f"{eng}{i % DMA_R}"
        w = [(sem, self.cnt[sem])] + list(waits)
        v = self.emit(eng, lambda e: e.dma_start(out_ap, in_ap),
                      waits=w, inc=sem, inc_amt=16)
        return (sem, v)

    def wait(self, eng, waits):
        for s, v in waits:
            assert v <= self.cnt[s]
        waits = tuple(waits)

        def thunk(e, waits=waits):
            for s, v in waits:
                e.wait_ge(self.sems[s], v)
        self.q[eng].append(thunk)

    def val(self, sem):
        return self.cnt[sem]

    def fence(self, writers, upstream):
        for w in writers:
            ws = [(s, v) for s, v in upstream if s != w and v > 0]
            if ws:
                self.wait(w, ws)

    def snap(self, engs=None):
        names = SEM_NAMES if engs is None else [
            s for s in SEM_NAMES if s[:2] in engs or s in engs]
        return [(s, self.cnt[s]) for s in names]


def host_prep(inputs):
    ca = np.ascontiguousarray
    d = {}
    d["wgT"] = ca(inputs["L0_ffn_w_gate"].T)
    d["wuT"] = ca(inputs["L0_ffn_w_up"].T)
    d["wdT"] = ca(inputs["L0_ffn_w_down"].T)
    d["wqT"] = ca(inputs["L0_attn_b_Wq"].T)
    d["wkT"] = ca(inputs["L0_attn_b_Wk"].T)
    d["wvT"] = ca(inputs["L0_attn_b_Wv"].T)
    d["wddT"] = ca(inputs["L0_attn_b_Wd"].T)
    d["bwT"] = ca(inputs["L0_attn_a_B_w"].T)
    d["cwT"] = ca(inputs["L0_attn_a_C_w"].T)
    Dp = np.asarray(inputs["L0_attn_a_D_param"], np.float32)
    dD = np.zeros((D, 128), np.float32)
    dD[np.arange(D), np.arange(D) % 128] = Dp
    d["dD"] = dD
    logA = np.asarray(inputs["L0_attn_a_log_A"], np.float32)
    d["logA"] = ca(np.broadcast_to(logA.reshape(128, 1), (128, TG)))

    def ppart(v):
        return ca(np.asarray(v, np.float32).reshape(DC, 128).T)
    d["bq_p"] = ppart(inputs["L0_attn_b_bq"])
    d["bd_p"] = ppart(inputs["L0_attn_b_bd"])
    d["gamma_p"] = ppart(inputs["L0_attn_b_gamma"])
    d["beta_p"] = ppart(inputs["L0_attn_b_beta"])
    d["bk_r"] = ca(np.asarray(inputs["L0_attn_b_bk"], np.float32).reshape(1, D))
    d["bv_r"] = ca(np.asarray(inputs["L0_attn_b_bv"], np.float32).reshape(1, D))
    d["ident"] = np.eye(128, dtype=np.float32)
    return d


P5LVL = 9

def build_program(upto=99):
    nc = bass.Bass("TRN2", target_bir_lowering=False)

    x_in = nc.dram_tensor("x_in", [L, D], F32, kind="ExternalInput")
    y_out = nc.dram_tensor("y_out", [L, D], F32, kind="ExternalOutput")
    wgT = nc.dram_tensor("wgT", [D, H], I32, kind="ExternalInput")
    wuT = nc.dram_tensor("wuT", [D, H], I32, kind="ExternalInput")
    wdT = nc.dram_tensor("wdT", [H, D], I32, kind="ExternalInput")
    wqT = nc.dram_tensor("wqT", [D, D], F32, kind="ExternalInput")
    wkT = nc.dram_tensor("wkT", [D, D], F32, kind="ExternalInput")
    wvT = nc.dram_tensor("wvT", [D, D], F32, kind="ExternalInput")
    wddT = nc.dram_tensor("wddT", [D, D], F32, kind="ExternalInput")
    bwT = nc.dram_tensor("bwT", [D, 128], F32, kind="ExternalInput")
    cwT = nc.dram_tensor("cwT", [128, D], F32, kind="ExternalInput")
    dD = nc.dram_tensor("dD", [D, 128], F32, kind="ExternalInput")
    logA = nc.dram_tensor("logA", [128, TG], F32, kind="ExternalInput")
    bq_p = nc.dram_tensor("bq_p", [128, DC], F32, kind="ExternalInput")
    bd_p = nc.dram_tensor("bd_p", [128, DC], F32, kind="ExternalInput")
    gamma_p = nc.dram_tensor("gamma_p", [128, DC], F32, kind="ExternalInput")
    beta_p = nc.dram_tensor("beta_p", [128, DC], F32, kind="ExternalInput")
    bk_r = nc.dram_tensor("bk_r", [1, D], F32, kind="ExternalInput")
    bv_r = nc.dram_tensor("bv_r", [1, D], F32, kind="ExternalInput")
    ident_in = nc.dram_tensor("ident_in", [128, 128], F32, kind="ExternalInput")
    sg = nc.dram_tensor("sg", [HC, 128, DC * 128], BF16, kind="Internal")
    su = nc.dram_tensor("su", [HC, 128, DC * 128], BF16, kind="Internal")
    sd = nc.dram_tensor("sd", [DC, 128, HC * 128], BF16, kind="Internal")
    sq = nc.dram_tensor("sq", [DC, 128, DC * 128], BF16, kind="Internal")
    sdd = nc.dram_tensor("sdd", [DC, 128, DC * 128], BF16, kind="Internal")

    st = ExitStack()
    with st:
        f32p = Pool(st.enter_context(nc.sbuf_tensor("f32p", [128, F32P_N], F32)),
                    AB_OFF)
        bf16p = Pool(st.enter_context(nc.sbuf_tensor("bf16p", [128, BF16P_N], BF16)),
                     BF16P_N)
        stream = f32p.alloc(DC * L)
        A_b = f32p.t[:, AB_OFF:AB_OFF + TG]
        bwcw = bf16p.t[:, BWCW_OFF:BWCW_OFF + 3 * D]
        bw_b = bwcw[:, 0:D]
        cw_b = bwcw[:, D:2 * D]
        dD_b = bwcw[:, 2 * D:3 * D]
        p1st = bf16p.t[:, P1ST_OFF:P1ST_OFF + 6144]

        ident_f = st.enter_context(nc.sbuf_tensor("ident_f", [128, 128], F32))
        ident_b = st.enter_context(nc.sbuf_tensor("ident_b", [128, 128], BF16))
        ones_b = st.enter_context(nc.sbuf_tensor("ones_b", [128, 128], BF16))
        onescol_b = st.enter_context(nc.sbuf_tensor("onescol_b", [128, 1], BF16))
        bk_b = st.enter_context(nc.sbuf_tensor("bk_b", [1, D], BF16))
        bv_b = st.enter_context(nc.sbuf_tensor("bv_b", [1, D], BF16))
        cvals = st.enter_context(nc.sbuf_tensor("cvals", [128, 32], F32))
        bq_s = st.enter_context(nc.sbuf_tensor("bq_s", [128, DC], F32))
        bd_s = st.enter_context(nc.sbuf_tensor("bd_s", [128, DC], F32))
        gam_s = st.enter_context(nc.sbuf_tensor("gam_s", [128, DC], F32))
        bet_s = st.enter_context(nc.sbuf_tensor("bet_s", [128, DC], F32))
        carry = cvals[:, 4:5]

        pb = [st.enter_context(nc.psum_tensor(f"pb{i}", [128, 512], F32))
              for i in range(8)]

        sems = {k: st.enter_context(nc.semaphore(f"sem_{k}")) for k in SEM_NAMES}
        S = Sched(nc, sems)

        def stream_ap(c, l0, n):
            return stream[:, c * L + l0: c * L + l0 + n]

        def stream_chunks_ap(l0, n, c0=0, cn=DC):
            r = stream.rearrange("p (c l) -> p c l", c=DC)
            return r[:, c0:c0 + cn, l0:l0 + n]

        S.emit("ve", lambda e: e.memset(ones_b[:], 1.0), inc="ve")
        S.emit("ve", lambda e: e.memset(cvals[:, 0:1], EPS_LN), inc="ve")
        S.emit("ve", lambda e: e.memset(cvals[:, 1:2], EPS_ILN), inc="ve")
        S.emit("ve", lambda e: e.memset(cvals[:, 2:3], -0.5), inc="ve")
        S.emit("ve", lambda e: e.memset(cvals[:, 5:6], -1.0), inc="ve")
        S.emit("ve", lambda e: e.memset(cvals[:, 6:7], QSCALE), inc="ve")
        neg1 = cvals[:, 5:6]
        qsc = cvals[:, 6:7]
        S.emit("ve", lambda e: e.memset(onescol_b[:], 1.0), inc="ve")
        c_id = S.dma("sy", ident_f[:], ident_in[:, :])
        S.dma("sy", bq_s[:], bq_p[:, :])
        S.dma("sy", bd_s[:], bd_p[:, :])
        S.dma("sy", gam_s[:], gamma_p[:, :])
        S.dma("sy", bet_s[:], beta_p[:, :])
        c_la = S.dma("sy", A_b, logA[:, :])
        se_A = S.emit("se", lambda e: e.activation(A_b, A_b, AF.Exp),
                      waits=[c_la], inc="se")
        cst_se = S.emit("se", lambda e: e.activation(ident_b[:], ident_f[:], AF.Copy),
                        waits=[c_id], inc="se")
        S.dma("gp", bk_b[0:1, :], bk_r[:, :])
        S.dma("gp", bv_b[0:1, :], bv_r[:, :])
        S.dma("gp", bw_b[:, :].rearrange("p (k n) -> p k n", k=DC),
              bwT[:, :].rearrange("(k p) n -> p k n", p=128))
        S.dma("gp", cw_b[:, :], cwT[:, :])
        attw_gp = S.dma("gp", dD_b[:, :].rearrange("p (k n) -> p k n", k=DC),
                        dD[:, :].rearrange("(k p) n -> p k n", p=128))

        p1_ready = {"sg": [None] * HC, "su": [None] * HC, "sd": [None] * DC,
                    "sq": [None] * DC, "sdd": [None] * DC}
        stage = [p1st[:, 0:3072], p1st[:, 3072:6144]]
        prev_out = [None, None]
        bi = 0

        def p1_piece(src_t, dst, name, kchunks, cbatch, ksplit):
            nonlocal bi, prev_out
            cols = 128 * cbatch
            nch = dst.shape[0]
            src3 = src_t.rearrange("(k p) n -> p k n", p=128)
            for cb in range(nch // cbatch):
                b2 = bi % 2
                bi += 1
                stg = stage[b2][:, 0:kchunks * cols]
                stg3 = stg.rearrange("p (k n) -> p k n", n=cols)
                kq = kchunks // ksplit
                c_in = None
                for s2 in range(ksplit):
                    w = []
                    if s2 == 0 and prev_out[b2] is not None:
                        w.append(prev_out[b2])
                    c_in = S.dma(
                        "gp", stg3[:, s2 * kq:(s2 + 1) * kq, :],
                        src3[:, s2 * kq:(s2 + 1) * kq,
                             cb * cols:(cb + 1) * cols], waits=w)
                for j in range(cbatch):
                    c = cb * cbatch + j
                    o = S.dma("gp", dst[c, :, :],
                              stg3[:, :, j * 128:(j + 1) * 128],
                              waits=[c_in] if j == 0 else ())
                    p1_ready[name][c] = o
                prev_out[b2] = p1_ready[name][cb * cbatch + cbatch - 1]

        if upto >= 3:
            p1_piece(wgT, sg, "sg", DC, 3, 1)
            p1_piece(wuT, su, "su", DC, 3, 1)
            p1_piece(wdT, sd, "sd", HC, 1, 3)
        if upto >= 5:
            p1_piece(wqT, sq, "sq", DC, 2, 1)
            p1_piece(wddT, sdd, "sdd", DC, 2, 1)
        p1_se = S.val("se")

        f32p.push()
        xt = [f32p.alloc(D) for _ in range(2)]
        p0_ve = {}
        for t in range(NT):
            b = t % 2
            w = [("ve", p0_ve[t - 2])] if t >= 2 else []
            c_in = S.dma("sy", xt[b][:, :], x_in[t * 128:(t + 1) * 128, :], waits=w)
            tw = [c_in, ("se", cst_se)] + (
                [("ve", p0_ve[t - 2])] if t >= 2 else [])
            for half in range(2):
                bank = pb[2 * b + half]

                def mk(tt=t, bb=b, hh=half, bank=bank):
                    def f(e):
                        ins = None
                        for j in range(4):
                            c = hh * 4 + j
                            ins = e.transpose(bank[:, j * 128:(j + 1) * 128],
                                              xt[bb][:, c * 128:(c + 1) * 128],
                                              ident_f[:])
                        return ins
                    return f
                c_te = S.emit("te", mk(), waits=tw if half == 0 else (), inc="te")

                def mkev(tt=t, hh=half, bank=bank):
                    def f(e):
                        dst = stream_chunks_ap(tt * 128, 128, c0=hh * 4, cn=4)
                        src = bank[:, :].rearrange("p (c l) -> p c l", c=4)
                        return e.tensor_copy(dst, src)
                    return f
                p0_ve[t] = S.emit("ve", mkev(), waits=[("te", c_te)], inc="ve")
        p0_snap = S.snap(("sy", "te", "ve"))
        p0_done_ve = S.val("ve")
        f32p.pop()

        def ln_stats(g, xsrc_chunks, sc, eps, red_banks, pre_w_se=(), pre_w_te=()):
            bA, bB = pb[red_banks[0]], pb[red_banks[1]]
            prev = sc.get("prev", {})
            se_c = ve_c = te_c = None
            x2_te_prev = prev.get("x2_te")
            tex_prev = prev.get("red_te")
            sq_prev = None
            for c in range(DC):
                w_se = list(pre_w_se) if c == 0 else []
                if tex_prev is not None:
                    w_se.append(("te", tex_prev))
                if c == 0 and "apply_ve" in prev:
                    w_se.append(("ve", prev["apply_ve"]))
                if sq_prev is not None:
                    w_se.append(("ve", sq_prev))
                se_c = S.emit(
                    "se",
                    (lambda cc: lambda e: e.activation(
                        sc["x_bf"], xsrc_chunks(cc), AF.Copy))(c),
                    waits=w_se, inc="se")
                w_ve = [("se", se_c)]
                if x2_te_prev is not None:
                    w_ve.append(("te", x2_te_prev))
                sq_prev = ve_c = S.emit(
                    "ve",
                    lambda e: e.tensor_tensor(sc["x2"], sc["x_bf"], sc["x_bf"],
                                              ALU.mult),
                    waits=w_ve, inc="ve")
                w = [("se", se_c)]
                if c == 0:
                    w += list(pre_w_te)
                    if "stats_ve" in prev:
                        w.append(("ve", prev["stats_ve"]))
                tex_prev = S.emit(
                    "te",
                    (lambda cc: lambda e: e.matmul(
                        bA[:, :], ones_b[:], sc["x_bf"],
                        start=(cc == 0), stop=(cc == DC - 1)))(c),
                    waits=w, inc="te")
                te_c = S.emit(
                    "te",
                    (lambda cc: lambda e: e.matmul(
                        bB[:, :], ones_b[:], sc["x2"],
                        start=(cc == 0), stop=(cc == DC - 1)))(c),
                    waits=[("ve", ve_c)], inc="te")
                x2_te_prev = te_c
            red_te = te_c
            ve_c = S.emit("ve", lambda e: e.tensor_scalar(
                sc["mu"], bA[:, 0:TG], 1.0 / D, None, ALU.mult),
                waits=[("te", red_te)], inc="ve")
            ve_c = S.emit("ve", lambda e: e.tensor_tensor(
                sc["s2"], sc["mu"], sc["mu"], ALU.mult), inc="ve")
            stats_ve = S.emit("ve", lambda e: e.scalar_tensor_tensor(
                sc["va"], bB[:, 0:TG], 1.0 / D, sc["s2"], ALU.mult, ALU.subtract),
                waits=[("te", red_te)], inc="ve")
            eps_ap = cvals[:, 0:1] if eps == EPS_LN else cvals[:, 1:2]
            se_c = S.emit("se", lambda e: e.activation(
                sc["va"], sc["va"], AF.Ln, bias=eps_ap),
                waits=[("ve", stats_ve)], inc="se")
            r_se = S.emit("se", lambda e: e.activation(
                sc["r_bf"], sc["va"], AF.Exp, scale=cvals[:, 2:3]), inc="se")
            s_ve = S.emit("ve", lambda e: e.tensor_tensor(
                sc["s_bf"], sc["mu"], sc["r_bf"], ALU.mult),
                waits=[("se", r_se)], inc="ve")
            sc["prev"] = {"red_te": red_te, "x2_te": x2_te_prev,
                          "stats_ve": stats_ve}
            return {"r_se": r_se, "s_ve": s_ve, "red_te": red_te,
                    "stats_ve": stats_ve}

        def ln_apply(st_r, xsrc_chunks, xn_dst, sc, apply_w=()):
            ve_c = None
            for c in range(DC):
                w_se = [("ve", st_r["s_ve"])] if c == 0 else [("ve", ve_c)]
                last_cast = S.emit(
                    "se",
                    (lambda cc: lambda e: e.activation(
                        sc["xa_bf"], xsrc_chunks(cc), AF.Copy))(c),
                    waits=w_se, inc="se")
                S.emit("ve",
                       (lambda cc: lambda e: e.tensor_tensor(
                           sc["t_bf"], sc["xa_bf"], sc["r_bf"],
                           ALU.mult))(c),
                       waits=[("se", last_cast)] + (list(apply_w) if c == 0 else []),
                       inc="ve")
                ve_c = S.emit("ve",
                              (lambda cc: lambda e: e.tensor_tensor(
                                  xn_dst(cc), sc["t_bf"], sc["s_bf"],
                                  ALU.subtract))(c),
                              inc="ve")
            sc["prev"]["apply_ve"] = ve_c
            return ve_c

        def ln_group(g, xsrc_chunks, xn_dst, sc, eps, red_banks,
                     pre_w_se=(), pre_w_te=(), apply_w=()):
            st_r = ln_stats(g, xsrc_chunks, sc, eps, red_banks, pre_w_se, pre_w_te)
            ve_c = ln_apply(st_r, xsrc_chunks, xn_dst, sc, apply_w)
            return {"te": st_r["red_te"], "apply_ve": ve_c, "se": st_r["r_se"],
                    "stats_ve": st_r["stats_ve"]}

        def ln_scratch(pool_b, pool_f):
            return {
                "x_bf": pool_b.alloc(TG),
                "xa_bf": pool_b.alloc(TG),
                "x2": pool_b.alloc(TG),
                "t_bf": pool_b.alloc(TG),
                "r_bf": pool_b.alloc(TG),
                "s_bf": pool_b.alloc(TG),
                "mu": pool_f.alloc(TG),
                "va": pool_f.alloc(TG),
                "s2": pool_f.alloc(TG),
            }

        last_ve = p0_done_ve
        if upto >= 2:
            f32p.push(); bf16p.push()
            S.fence(("sy", "te", "ve", "se"), p0_snap)
            sc_a = ln_scratch(bf16p, f32p)
            sc_b = {
                "x_bf": bf16p.alloc(TG),
                "xa_bf": bf16p.alloc(TG),
                "x2": bf16p.alloc(TG),
                "t_bf": bf16p.alloc(TG),
                "r_bf": bf16p.alloc(TG),
                "s_bf": bf16p.alloc(TG),
                "mu": sc_a["mu"], "s2": sc_a["s2"],
                "va": f32p.alloc(TG),
            }
            sc2 = [sc_a, sc_b]
            xn1 = bf16p.alloc(DC * (L // 2))
            hs_g = [bf16p.alloc(TG) for _ in range(2)]

            def xn1_ap(c, g, n=TG):
                gg = g % 4
                return xn1[:, c * (L // 2) + gg * TG: c * (L // 2) + gg * TG + n]

            scan_prev = None
            yev_ve = {}
            ylast = {}
            y_te_last = {}
            yend_ve = yend_te = None
            for h in range(2):
                xp_te = {}
                scan_ve = {}
                for g4 in range(4):
                    g = h * 4 + g4
                    pre_te = [("ve", yend_ve)] if (h > 0 and g4 <= 1) else []
                    app_w = [("te", yend_te)] if (h > 0 and g4 == 0) else []
                    r = ln_group(
                        g, lambda c, gg=g: stream_ap(c, gg * TG, TG),
                        lambda c, gg=g: xn1_ap(c, gg), sc2[g % 2], EPS_LN,
                        red_banks=(4, 5) if g % 2 == 0 else (6, 7),
                        pre_w_te=pre_te, apply_w=app_w)
                    for k in range(DC):
                        w = []
                        if k == 0:
                            w = [("ve", r["apply_ve"]), attw_gp]
                            if h > 0:
                                w.append(("ve", scan_prev[g4]))
                        xp_te[g4] = S.emit(
                            "te",
                            (lambda kk, gg4=g4, gg=g: lambda e: e.matmul(
                                pb[gg4][:, :], bw_b[:, kk * 128:(kk + 1) * 128],
                                xn1_ap(kk, gg),
                                start=(kk == 0), stop=(kk == DC - 1)))(k),
                            waits=w, inc="te" if k == DC - 1 else None)
                for g4 in range(4):
                    g = h * 4 + g4
                    if g == 0:
                        init = 0.0
                    else:
                        init = hs_g[(g4 - 1) % 2][:, TG - 1:TG]
                    w = [("se", se_A), ("te", xp_te[g4])]
                    if g - 2 in y_te_last:
                        w.append(("te", y_te_last[g - 2]))
                    scan_ve[g4] = S.emit(
                        "ve",
                        (lambda gg4=g4, init=init: lambda e: e.tensor_tensor_scan(
                            hs_g[gg4 % 2][:, :], A_b, pb[gg4][:, :], init,
                            ALU.mult, ALU.add))(g4),
                        waits=w, inc="ve")
                    for c in range(DC):
                        bank = pb[4 + (c % 4)]
                        w = [("ve", scan_ve[g4])] if c == 0 else []
                        key = (g, c)
                        prev = (g, c - 4) if c >= 4 else (
                            (g - 1, c + 4) if g > 0 else None)
                        if prev in yev_ve:
                            w.append(("ve", yev_ve[prev]))
                        S.emit("te",
                               (lambda cc, bank=bank, gg4=g4: lambda e: e.matmul(
                                   bank[:, :], cw_b[:, cc * 128:(cc + 1) * 128],
                                   hs_g[gg4 % 2][:, :], start=True, stop=False))(c),
                               waits=w, inc=None)
                        te_c = S.emit(
                            "te",
                            (lambda cc, gg=g, bank=bank: lambda e: e.matmul(
                                bank[:, :], dD_b[:, cc * 128:(cc + 1) * 128],
                                xn1_ap(cc, gg), start=False, stop=True))(c),
                            inc="te")
                        y_te_last[g] = te_c
                        yev_ve[key] = S.emit(
                            "ve",
                            (lambda cc, gg=g, bank=bank: lambda e: e.tensor_tensor(
                                stream_ap(cc, gg * TG, TG), bank[:, :],
                                stream_ap(cc, gg * TG, TG), ALU.add))(c),
                            waits=[("te", te_c)], inc="ve")
                scan_prev = scan_ve
                yend_ve = S.val("ve")
                yend_te = S.val("te")
            last_ve = S.val("ve")
            f32p.pop(); bf16p.pop()

        def ffn_phase(upstream):
            f32p.push(); bf16p.push()
            S.fence(("sy", "te", "ve", "se"), upstream)
            sc = ln_scratch(bf16p, f32p)
            xn2 = [bf16p.alloc(DC * TG) for _ in range(2)]
            hbuf = bf16p.alloc(HC * TG)
            sgt = bf16p.alloc(TG)
            wg_buf = [bf16p.alloc(DC * 128) for _ in range(2)]
            wu_buf = [bf16p.alloc(DC * 128) for _ in range(2)]
            wd_buf = [bf16p.alloc(HC * 64) for _ in range(2)]

            gu_te = {}
            gu_dma = {}
            silu_se = {}
            mult_ve = {}
            dn_ev = {}
            dn_te = {}
            dn_half_te = {}
            apply_done = {}

            def emit_ln(g):
                aw = []
                if ("u", HC - 1, g - 2) in gu_te:
                    aw.append(("te", gu_te[("u", HC - 1, g - 2)]))
                r = ln_group(
                    g, lambda c, gg=g: stream_ap(c, gg * TG, TG),
                    lambda c, gg=g: xn2[gg % 2][:, c * TG:(c + 1) * TG],
                    sc, EPS_LN, red_banks=(6, 7), apply_w=aw)
                apply_done[g] = r["apply_ve"]

            emit_ln(0)
            for g in range(NG):
                if g + 1 < NG:
                    emit_ln(g + 1)
                xn = xn2[g % 2]
                for c in range(HC):
                    b = c % 2
                    w = [p1_ready["sg"][c]]
                    if ("g", c - 2, g) in gu_te:
                        w.append(("te", gu_te[("g", c - 2, g)]))
                    elif g > 0 and c < 2:
                        w.append(("te", gu_te[("g", HC - 2 + c, g - 1)]))
                    gu_dma[("g", c)] = S.dma("sy", wg_buf[b][:, :], sg[c, :, :],
                                             waits=w)
                    w = [p1_ready["su"][c]]
                    if ("u", c - 2, g) in gu_te:
                        w.append(("te", gu_te[("u", c - 2, g)]))
                    elif g > 0 and c < 2:
                        w.append(("te", gu_te[("u", HC - 2 + c, g - 1)]))
                    gu_dma[("u", c)] = S.dma("sy", wu_buf[b][:, :], su[c, :, :],
                                             waits=w)
                    for k in range(DC):
                        w = []
                        if k == 0:
                            w = [gu_dma[("g", c)], ("ve", apply_done[g])]
                            if c >= 2:
                                w.append(("se", silu_se[c - 2]))
                            elif g > 0:
                                w.append(("se", silu_se[HC - 2 + c]))
                        gu_te[("g", c, g)] = S.emit(
                            "te",
                            (lambda kk, cc=c, bb=b, xn=xn: lambda e: e.matmul(
                                pb[bb][:, :], wg_buf[bb][:, kk * 128:(kk + 1) * 128],
                                xn[:, kk * TG:(kk + 1) * TG],
                                start=(kk == 0), stop=(kk == DC - 1)))(k),
                            waits=w, inc="te" if k == DC - 1 else None)
                    for k in range(DC):
                        w = []
                        if k == 0:
                            w = [gu_dma[("u", c)]]
                            if c >= 2:
                                w.append(("ve", mult_ve[c - 2]))
                            elif g > 0:
                                w.append(("ve", mult_ve[HC - 2 + c]))
                        gu_te[("u", c, g)] = S.emit(
                            "te",
                            (lambda kk, cc=c, bb=b, xn=xn: lambda e: e.matmul(
                                pb[2 + bb][:, :], wu_buf[bb][:, kk * 128:(kk + 1) * 128],
                                xn[:, kk * TG:(kk + 1) * TG],
                                start=(kk == 0), stop=(kk == DC - 1)))(k),
                            waits=w, inc="te" if k == DC - 1 else None)
                    w = [("te", gu_te[("g", c, g)])]
                    if c >= 1:
                        w.append(("ve", mult_ve[c - 1]))
                    elif g > 0:
                        w.append(("ve", mult_ve[HC - 1]))
                    silu_se[c] = S.emit(
                        "se",
                        (lambda bb=b: lambda e: e.activation(
                            sgt, pb[bb][:, :], AF.Silu, scale=qsc))(),
                        waits=w, inc="se")
                    mult_ve[c] = S.emit(
                        "ve",
                        (lambda cc=c, bb=b: lambda e: e.tensor_tensor(
                            hbuf[:, cc * TG:(cc + 1) * TG], pb[2 + bb][:, :],
                            sgt, ALU.mult))(c),
                        waits=[("te", gu_te[("u", c, g)]), ("se", silu_se[c])],
                        inc="ve")
                HH = HC // 2
                for cd in range(DC):
                    for hf in range(2):
                        hidx = 2 * cd + hf
                        bb2 = hidx % 2
                        w = [p1_ready["sd"][cd]]
                        if (g, hidx) != (0, 0):
                            key = (hidx - 2) % 16
                            if key in dn_half_te:
                                w.append(("te", dn_half_te[key]))
                        c_dma = S.dma(
                            "sy", wd_buf[bb2][:, :],
                            sd[cd, :, :][:, hf * HH * 128:(hf + 1) * HH * 128],
                            waits=w)
                        for k2 in range(HH):
                            k = hf * HH + k2
                            w = []
                            if k2 == 0:
                                w = [c_dma]
                                if hf == 0:
                                    w.append(("ve", mult_ve[HC - 1]))
                                    if cd >= 2:
                                        w.append(("ve", dn_ev[cd - 2]))
                                    elif g > 0:
                                        w.append(("ve", dn_ev[DC - 2 + cd]))
                            ins = S.emit(
                                "te",
                                (lambda kk, k2=k2, bb2=bb2, cc=cd: lambda e: e.matmul(
                                    pb[4 + (cc % 2)][:, :],
                                    wd_buf[bb2][:, k2 * 128:(k2 + 1) * 128],
                                    hbuf[:, kk * TG:(kk + 1) * TG],
                                    start=(kk == 0), stop=(kk == HC - 1)))(k),
                                waits=w,
                                inc="te" if k2 == HH - 1 else None)
                            if ins is not None:
                                dn_half_te[hidx % 16] = ins
                                dn_te[cd] = ins
                    dn_ev[cd] = S.emit(
                        "ve",
                        (lambda cc=cd, gg=g: lambda e: e.scalar_tensor_tensor(
                            stream_ap(cc, gg * TG, TG), pb[4 + (cc % 2)][:, :],
                            QSCALE * QSCALE, stream_ap(cc, gg * TG, TG),
                            ALU.mult, ALU.add))(cd),
                        waits=[("te", dn_te[cd])], inc="ve")
            out = S.val("ve")
            f32p.pop(); bf16p.pop()
            return out

        if upto >= 3:
            last_ve = ffn_phase(S.snap())

        def p4_phase(kvks):
            f32p.push(); bf16p.push()
            S.fence(("sy", "gp", "te", "ve", "se"), S.snap())
            sc = ln_scratch(bf16p, f32p)
            xn3db = [bf16p.alloc(DC * TG) for _ in range(2)]
            wk_bf = bf16p.alloc(DC * D)
            wv_bf = bf16p.alloc(DC * D)
            k1_bf = bf16p.alloc(D)
            v1_bf = bf16p.alloc(D)
            t1_bf = bf16p.alloc(TG)
            e_bf = bf16p.alloc(TG)

            wkv_gp = None
            for k in range(DC):
                S.dma("gp", wk_bf[:, k * D:(k + 1) * D],
                      wkT[k * 128:(k + 1) * 128, :])
                wkv_gp = S.dma("gp", wv_bf[:, k * D:(k + 1) * D],
                               wvT[k * 128:(k + 1) * 128, :])
            ksum_acc = f32p.alloc(8)
            S.emit("ve", lambda e: e.memset(kvks[:, :], 0.0), inc="ve")
            S.emit("ve", lambda e: e.memset(ksum_acc, 0.0), inc="ve")
            kv_mm_last = None
            ksum_mm_last = None
            elu_ve_prev = None
            vev_se_prev = None
            vev0 = vev1 = None
            ksacc_ve = None
            apply_done = {}
            last_proj_te = {}

            def emit_ln4(g):
                aw = []
                if g - 2 in last_proj_te:
                    aw.append(("te", last_proj_te[g - 2]))
                r = ln_group(
                    g, lambda c, gg=g: stream_ap(c, gg * TG, TG),
                    lambda c, gg=g: xn3db[gg % 2][:, c * TG:(c + 1) * TG],
                    sc, EPS_LN, red_banks=(6, 7), apply_w=aw)
                apply_done[g] = r["apply_ve"]

            emit_ln4(0)
            for g in range(NG):
                if g + 1 < NG:
                    emit_ln4(g + 1)
                xn3 = xn3db[g % 2]
                r = {"apply_ve": apply_done[g]}
                for t in range(4):
                    T = g * 4 + t
                    for half in range(2):
                        for k in range(DC):
                            w = []
                            if half == 0 and k == 0:
                                w = [("ve", r["apply_ve"]), wkv_gp]
                                if elu_ve_prev is not None:
                                    w.append(("ve", elu_ve_prev))
                                if ksacc_ve is not None:
                                    w.append(("ve", ksacc_ve))
                            S.emit(
                                "te",
                                (lambda kk, hh=half, gg=g, tt=t, xn3=xn3: lambda e: e.matmul(
                                    pb[hh][:, :],
                                    xn3[:, kk * TG + tt * 128: kk * TG + (tt + 1) * 128],
                                    wk_bf[:, kk * D + hh * 512: kk * D + (hh + 1) * 512],
                                    start=(kk == 0), stop=False))(k),
                                waits=w, inc=None)
                        kp_te = S.emit(
                            "te",
                            (lambda hh=half: lambda e: e.matmul(
                                pb[hh][:, :], ones_b[0:1, 0:128],
                                bk_b[0:1, hh * 512:(hh + 1) * 512],
                                start=False, stop=True))(),
                            inc="te")
                        w = [("te", kp_te)]
                        if elu_ve_prev is not None:
                            w.append(("ve", elu_ve_prev))
                        if half == 0 and ksum_mm_last is not None:
                            w.append(("te", ksum_mm_last))
                        se1 = S.emit(
                            "se",
                            (lambda hh=half: lambda e: e.activation(
                                t1_bf, pb[hh][:, :], AF.Relu, scale=neg1))(),
                            waits=w, inc="se")
                        se2 = S.emit(
                            "se",
                            lambda e: e.activation(e_bf, t1_bf, AF.Exp, scale=neg1),
                            inc="se")
                        elu_ve_prev = S.emit(
                            "ve",
                            (lambda hh=half: lambda e: e.scalar_tensor_tensor(
                                k1_bf[:, hh * 512:(hh + 1) * 512], pb[hh][:, :],
                                0.0, e_bf, ALU.max, ALU.add))(),
                            waits=[("se", se2)], inc="ve")
                    for half in range(2):
                        for k in range(DC):
                            w = []
                            if k == 0 and half == 0 and vev0 is not None:
                                w = [vev0, vev1]
                            S.emit(
                                "te",
                                (lambda kk, hh=half, gg=g, tt=t, xn3=xn3: lambda e: e.matmul(
                                    pb[2 + hh][:, :],
                                    xn3[:, kk * TG + tt * 128: kk * TG + (tt + 1) * 128],
                                    wv_bf[:, kk * D + hh * 512: kk * D + (hh + 1) * 512],
                                    start=(kk == 0), stop=False))(k),
                                waits=w, inc=None)
                        vp_te = S.emit(
                            "te",
                            (lambda hh=half: lambda e: e.matmul(
                                pb[2 + hh][:, :], ones_b[0:1, 0:128],
                                bv_b[0:1, hh * 512:(hh + 1) * 512],
                                start=False, stop=True))(),
                            inc="te")
                        last_proj_te[g] = vp_te
                        w = [("te", vp_te)]
                        if half == 0 and kv_mm_last is not None:
                            w.append(("te", kv_mm_last))
                        if half == 0:
                            vev_se_prev = S.emit(
                                "se",
                                (lambda hh=half: lambda e: e.activation(
                                    v1_bf[:, hh * 512:(hh + 1) * 512],
                                    pb[2 + hh][:, :], AF.Copy))(),
                                waits=w, inc="se")
                            vev0 = ("se", vev_se_prev)
                        else:
                            vev1 = ("ve", S.emit(
                                "ve",
                                (lambda hh=half: lambda e: e.tensor_copy(
                                    v1_bf[:, hh * 512:(hh + 1) * 512],
                                    pb[2 + hh][:, :]))(),
                                waits=w, inc="ve"))
                    w0 = [vev0, vev1, ("ve", elu_ve_prev)]
                    for p in range(NH // 2):
                        bank = pb[4 + p // 4]
                        base = (p % 4) * 128
                        for odd in range(2):
                            hh = 2 * p + odd
                            tp = (0, 64 * odd)
                            po = 64 * odd
                            lhs = k1_bf[:, hh * 64:(hh + 1) * 64]
                            w = w0 if (p == 0 and odd == 0) else ()
                            last_pair = (p == NH // 2 - 1 and odd == 1)
                            kv_mm_last = S.emit(
                                "te",
                                (lambda lhs=lhs, bank=bank, base=base, po=po, tp=tp,
                                 hh=hh, TT=T: lambda e: e.matmul(
                                     bank[po:po + 64, base + po:base + po + 64], lhs,
                                     v1_bf[:, hh * 64:(hh + 1) * 64],
                                     start=(TT == 0), stop=(TT == NT - 1),
                                     tile_position=tp, skip_group_check=True))(),
                                waits=w,
                                inc="te" if last_pair else None) or kv_mm_last
                    for p in range(NH // 2):
                        for odd in range(2):
                            hh = 2 * p + odd
                            tp = (0, 64 * odd)
                            po = 64 * odd
                            w = []
                            if p == 0 and odd == 0:
                                w = [("ve", elu_ve_prev)]
                            last_pair = (p == NH // 2 - 1 and odd == 1)
                            ksum_mm_last = S.emit(
                                "te",
                                (lambda hh=hh, po=po, tp=tp, p=p: lambda e: e.matmul(
                                    pb[0][po:po + 64, 496 + p:497 + p],
                                    k1_bf[:, hh * 64:(hh + 1) * 64], onescol_b[:, :],
                                    start=True, stop=True,
                                    tile_position=tp, skip_group_check=True))(),
                                waits=w,
                                inc="te" if last_pair else None) or ksum_mm_last
                    wks = [("te", ksum_mm_last)]
                    ksacc_ve = S.emit(
                        "ve",
                        lambda e: e.tensor_tensor(ksum_acc, pb[0][:, 496:504],
                                                  ksum_acc, ALU.add),
                        waits=wks, inc="ve")
            kv_ev = None
            for p in range(NH // 2):
                bank = pb[4 + p // 4]
                base = (p % 4) * 128
                for odd in range(2):
                    po = 64 * odd
                    w = [("te", kv_mm_last)] if kv_ev is None else []
                    S.emit(
                        "ve",
                        (lambda bank=bank, base=base, po=po, p=p: lambda e: e.tensor_copy(
                            kvks[po:po + 64, p * 130 + po: p * 130 + po + 64],
                            bank[po:po + 64, base + po:base + po + 64]))(),
                        waits=w, inc="ve")
                    kv_ev = S.emit(
                        "ve",
                        (lambda po=po, p=p, odd=odd: lambda e: e.tensor_scalar(
                            kvks[po:po + 64, p * 130 + 128 + odd: p * 130 + 129 + odd],
                            ksum_acc[po:po + 64, p:p + 1],
                            1e-6, None, ALU.add))(),
                        waits=[("ve", ksacc_ve)] if kv_ev is None else [],
                        inc="ve")
            f32p.pop(); bf16p.pop()
            return S.snap()

        def p5_phase(kvks, p4_snap):
            f32p.push(); bf16p.push()
            S.fence(("sy", "gp", "te", "ve", "se"), p4_snap)
            sc = ln_scratch(bf16p, f32p)
            xn3db = [bf16p.alloc(DC * TG) for _ in range(2)]
            q1 = bf16p.alloc(DC * TG)
            hs_bf = q1
            t1_bf = bf16p.alloc(TG)
            t2_bf = bf16p.alloc(TG)
            e_bf = bf16p.alloc(TG)
            ctx_bf = [bf16p.alloc(D) for _ in range(2)]
            ctx_fm = bf16p.alloc(DC * TG)
            qw_buf = [bf16p.alloc(DC * 128) for _ in range(2)]
            dw_buf = [bf16p.alloc(DC * 128) for _ in range(2)]
            z_f = cvals[:, 8:24]
            neg_bq = cvals[:, 24:32]
            ilnr = bf16p.alloc(TG)
            ilns = bf16p.alloc(TG)
            iln_va = f32p.alloc(TG)
            ta = [bf16p.alloc(TG) for _ in range(2)]
            tb = [bf16p.alloc(TG) for _ in range(2)]
            tc = ta
            assert f32p.off <= AB_OFF and bf16p.off <= KVKS_OFF, (
                f"P5 pools overlap tails: {f32p.off} {bf16p.off}")

            S.emit("ve", lambda e: e.tensor_scalar(
                neg_bq, bq_s[:, :], -1.0, None, ALU.mult), inc="ve")

            q_dma = {}
            q_te = {}
            q_ve = {}
            q_se3 = {}
            ctx_ts_ve = None
            tp_ve_prev = None
            hs_te = {}
            hs_ev = {}
            iln_red_te = None
            iln_apply_ve = None
            stream_ve_prev = None
            apply_done5 = {}
            q_last_te = {}

            def emit_ln5(g):
                aw = []
                if g - 2 in q_last_te:
                    aw.append(("te", q_last_te[g - 2]))
                r = ln_group(
                    g, lambda c, gg=g: stream_ap(c, gg * TG, TG),
                    lambda c, gg=g: xn3db[gg % 2][:, c * TG:(c + 1) * TG],
                    sc, EPS_LN, red_banks=(6, 7), apply_w=aw)
                apply_done5[g] = r["apply_ve"]

            def emit_iln_apply(st5, gg):
                q1_reads = {}
                tc_gp = {}
                for c in range(DC):
                    i = c % 2
                    w1 = [("ve", st5["s_ve"])] if c == 0 else []
                    if c >= 2:
                        w1.append(("gpc", tc_gp[c - 2]))
                    q1_reads[c] = S.emit(
                        "ve",
                        (lambda cc=c, i=i: lambda e: e.tensor_tensor(
                            ta[i], hs_bf[:, cc * TG:(cc + 1) * TG], ilnr,
                            ALU.mult))(),
                        waits=w1, inc="ve")
                    S.emit("ve",
                           (lambda cc=c, i=i: lambda e: e.tensor_tensor(
                               tb[i], ta[i], ilns, ALU.subtract))(),
                           inc="ve")
                    tb_ve = S.emit(
                        "ve",
                        (lambda cc=c, i=i: lambda e: e.tensor_scalar(
                            tc[i], tb[i], gam_s[:, cc:cc + 1],
                            bet_s[:, cc:cc + 1], ALU.mult, ALU.add))(),
                        inc="ve")
                    tc_gp[c] = S.emit(
                        "gp",
                        (lambda cc=c, i=i, gg=gg: lambda e: e.tensor_tensor(
                            stream_ap(cc, gg * TG, TG), tc[i],
                            stream_ap(cc, gg * TG, TG), ALU.add))(),
                        waits=[("ve", tb_ve)], inc="gpc")
                return q1_reads, tc_gp[DC - 1]

            pend = None
            pend_reads = None
            last_stream_gp = None
            emit_ln5(0)
            for g in range(NG):
                if g + 1 < NG:
                    emit_ln5(g + 1)
                if pend is not None:
                    pend_reads, last_stream_gp = emit_iln_apply(pend, g - 1)
                xn3 = xn3db[g % 2]
                r = {"apply_ve": apply_done5[g]}
                for c in range(DC):
                    b = c % 2
                    w = [p1_ready["sq"][c]]
                    if (g, c) != (0, 0):
                        prev_c = c - 2 if c >= 2 else DC - 2 + c
                        if (prev_c in q_te) and not (g == 0 and c < 2):
                            w.append(("te", q_te[prev_c]))
                    q_dma[c] = S.dma("sy", qw_buf[b][:, :], sq[c, :, :], waits=w)
                    for k in range(DC):
                        w = []
                        if k == 0:
                            w = [q_dma[c], ("ve", r["apply_ve"])]
                            prev_c = c - 2 if c >= 2 else (DC - 2 + c if g > 0 else None)
                            if prev_c is not None and prev_c in q_se3:
                                w.append(("se", q_se3[prev_c]))
                            if c <= 1 and (DC - 2 + c) in hs_ev and g > 0:
                                w.append(("ve", hs_ev[DC - 2 + c]))
                        q_te[c] = S.emit(
                            "te",
                            (lambda kk, cc=c, bb=b, xn3=xn3: lambda e: e.matmul(
                                pb[bb][:, :], qw_buf[bb][:, kk * 128:(kk + 1) * 128],
                                xn3[:, kk * TG:(kk + 1) * TG],
                                start=(kk == 0), stop=(kk == DC - 1)))(k),
                            waits=w, inc="te" if k == DC - 1 else None) or q_te.get(c)
                    se1 = S.emit(
                        "se",
                        (lambda cc=c, bb=b: lambda e: e.activation(
                            t1_bf, pb[bb][:, :], AF.Relu, scale=neg1,
                            bias=neg_bq[:, cc:cc + 1]))(),
                        waits=[("te", q_te[c])], inc="se")
                    se2 = S.emit(
                        "se",
                        lambda e: e.activation(e_bf, t1_bf, AF.Exp, scale=neg1),
                        inc="se")
                    se3 = S.emit(
                        "se",
                        (lambda cc=c, bb=b: lambda e: e.activation(
                            t2_bf, pb[bb][:, :], AF.Relu,
                            bias=bq_s[:, cc:cc + 1]))(),
                        inc="se")
                    q_se3[c] = se3
                    wqv = [("se", se3)]
                    if c == 0 and iln_red_te is not None:
                        wqv.append(("te", iln_red_te))
                    if pend_reads is not None:
                        wqv.append(("ve", pend_reads[c]))
                    q_ve[c] = ("gpc", S.emit(
                        "gp",
                        (lambda cc=c: lambda e: e.tensor_tensor(
                            q1[:, cc * TG:(cc + 1) * TG], t2_bf, e_bf, ALU.add))(),
                        waits=wqv, inc="gpc"))
                q_last_te[g] = q_te[DC - 1]
                ctx_z_se = None
                ctx_z_ve = None
                tp_te_prev2 = None
                for t in range(4):
                    den_mm = None
                    for p in range(NH // 2):
                        bank = pb[2 + p // 4]
                        off = (p % 4) * 128
                        w = []
                        if p == 0:
                            w = [q_ve[DC - 1]]
                            if ctx_z_se is not None:
                                w.append(("se", ctx_z_se))
                        elif p == 4 and ctx_z_ve is not None:
                            w = [("ve", ctx_z_ve)]
                        S.emit(
                            "te",
                            (lambda pp=p, tt=t, bank=bank, off=off: lambda e: e.matmul(
                                bank[:, off:off + 128],
                                q1[:, pp * TG + tt * 128: pp * TG + (tt + 1) * 128],
                                kvks[:, pp * 130: pp * 130 + 128],
                                start=True, stop=True))(),
                            waits=w, inc=None)
                        den_mm = S.emit(
                            "te",
                            (lambda pp=p, tt=t: lambda e: e.matmul(
                                pb[4][:, pp * 2:(pp + 1) * 2],
                                q1[:, pp * TG + tt * 128: pp * TG + (tt + 1) * 128],
                                kvks[:, pp * 130 + 128: pp * 130 + 130],
                                start=True, stop=True))(),
                            inc="te" if p == NH // 2 - 1 else None) or den_mm
                    w = [("te", den_mm)]
                    if ctx_z_se is not None:
                        w.append(("se", ctx_z_se))
                    rec_ve = S.emit(
                        "ve", lambda e: e.reciprocal(z_f, pb[4][:, 0:16]),
                        waits=w, inc="ve")
                    for hh in range(8):
                        bank = pb[2]
                        off = hh * 64
                        w = [("ve", rec_ve)] if hh == 0 else []
                        if hh == 0 and tp_te_prev2 is not None:
                            w.append(("te", tp_te_prev2))
                        ctx_z_se = S.emit(
                            "se",
                            (lambda hh=hh, tt=t, bank=bank, off=off: lambda e:
                             e.activation(
                                 ctx_bf[tt % 2][:, hh * 64:(hh + 1) * 64],
                                 bank[:, off:off + 64], AF.Identity,
                                 scale=z_f[:, hh:hh + 1]))(),
                            waits=w, inc="se")
                    for hh in range(8, NH):
                        bank = pb[3]
                        off = (hh - 8) * 64
                        w = []
                        if hh == 8 and tp_te_prev2 is not None:
                            w.append(("te", tp_te_prev2))
                        ctx_z_ve = S.emit(
                            "ve",
                            (lambda hh=hh, tt=t, bank=bank, off=off: lambda e:
                             e.tensor_scalar(
                                 ctx_bf[tt % 2][:, hh * 64:(hh + 1) * 64],
                                 bank[:, off:off + 64], z_f[:, hh:hh + 1],
                                 None, ALU.mult))(),
                            waits=w, inc="ve")
                    for rnd in range(2):
                        w = ([("se", ctx_z_se)] if rnd == 0
                             else [("ve", ctx_z_ve)])
                        if tp_ve_prev is not None:
                            w.append(("ve", tp_ve_prev))

                        def mkt(tt=t, rr=rnd):
                            def f(e):
                                ins = None
                                for j in range(4):
                                    m = rr * 4 + j
                                    ins = e.matmul(
                                        pb[5][:, j * 128:(j + 1) * 128],
                                        ctx_bf[tt % 2][:, m * 128:(m + 1) * 128],
                                        ident_b[:], start=True, stop=True)
                                return ins
                            return f
                        tp_te = S.emit("te", mkt(), waits=w, inc="te")
                        tp_te_prev2 = tp_te

                        def mke(tt=t, rr=rnd):
                            def f(e):
                                dst = ctx_fm.rearrange(
                                    "p (m l) -> p m l", m=DC)[:, rr * 4:(rr + 1) * 4,
                                                             tt * 128:(tt + 1) * 128]
                                src = pb[5][:, :].rearrange("p (m l) -> p m l", m=4)
                                return e.tensor_copy(dst, src)
                            return f
                        tp_ve_prev = S.emit("ve", mke(), waits=[("te", tp_te)],
                                            inc="ve")
                for cd in range(DC):
                    b = cd % 2
                    w = [p1_ready["sdd"][cd]]
                    if cd >= 2:
                        w.append(("te", hs_te[cd - 2]))
                    elif g > 0:
                        w.append(("te", hs_te[DC - 2 + cd]))
                    c_dma = S.dma("sy", dw_buf[b][:, :], sdd[cd, :, :], waits=w)
                    for m in range(DC):
                        w = []
                        if m == 0:
                            w = [c_dma, ("ve", tp_ve_prev)]
                            if cd >= 2:
                                w.append(("ve", hs_ev[cd - 2]))
                            else:
                                w.append(("se", q_se3[DC - 2 + cd]))
                        hs_te[cd] = S.emit(
                            "te",
                            (lambda mm, cc=cd, bb=b: lambda e: e.matmul(
                                pb[bb][:, :], dw_buf[bb][:, mm * 128:(mm + 1) * 128],
                                ctx_fm[:, mm * TG:(mm + 1) * TG],
                                start=(mm == 0), stop=(mm == DC - 1)))(m),
                            waits=w, inc="te" if m == DC - 1 else None) or hs_te.get(cd)
                    wev = [("te", hs_te[cd])]
                    if pend is not None:
                        pass
                    hs_ev[cd] = S.emit(
                        "ve",
                        (lambda cc=cd, bb=b, xn3=xn3: lambda e: e.scalar_tensor_tensor(
                            hs_bf[:, cc * TG:(cc + 1) * TG], pb[bb][:, :],
                            bd_s[:, cc:cc + 1], xn3[:, cc * TG:(cc + 1) * TG],
                            ALU.add, ALU.add))(),
                        waits=wev, inc="ve")
                prev = sc.get("prev", {})
                x2_te_prev2 = prev.get("x2_te")
                sq_prev = None
                te_c = None
                x2m = {}
                x2_te_prev3 = prev.get("x2_te")
                for c in range(DC):
                    x2s = sc["x2"]
                    w_ve = [("ve", hs_ev[c])]
                    if x2_te_prev3 is not None:
                        w_ve.append(("te", x2_te_prev3))
                    sq_prev = S.emit(
                        "ve",
                        (lambda cc=c, x2s=x2s: lambda e: e.tensor_tensor(
                            x2s, hs_bf[:, cc * TG:(cc + 1) * TG],
                            hs_bf[:, cc * TG:(cc + 1) * TG], ALU.mult))(),
                        waits=w_ve, inc="ve")
                    w = [("ve", hs_ev[c])]
                    if c == 0 and "stats_ve" in prev:
                        w.append(("ve", prev["stats_ve"]))
                    S.emit(
                        "te",
                        (lambda cc=c: lambda e: e.matmul(
                            pb[6][:, :], ones_b[:], hs_bf[:, cc * TG:(cc + 1) * TG],
                            start=(cc == 0), stop=(cc == DC - 1)))(c),
                        waits=w, inc="te")
                    te_c = S.emit(
                        "te",
                        (lambda cc=c, x2s=x2s: lambda e: e.matmul(
                            pb[7][:, :], ones_b[:], x2s,
                            start=(cc == 0), stop=(cc == DC - 1)))(c),
                        waits=[("ve", sq_prev)], inc="te")
                    x2_te_prev3 = te_c
                x2_te_prev2 = te_c
                iln_red_te = te_c
                ve_c = S.emit("ve", lambda e: e.tensor_scalar(
                    sc["mu"], pb[6][:, 0:TG], 1.0 / D, None, ALU.mult),
                    waits=[("te", iln_red_te)], inc="ve")
                ve_c = S.emit("ve", lambda e: e.tensor_tensor(
                    sc["s2"], sc["mu"], sc["mu"], ALU.mult), inc="ve")
                stats_ve = S.emit("ve", lambda e: e.scalar_tensor_tensor(
                    iln_va, pb[7][:, 0:TG], 1.0 / D, sc["s2"], ALU.mult,
                    ALU.subtract),
                    waits=[("te", iln_red_te)], inc="ve")
                se_c = S.emit("se", lambda e: e.activation(
                    iln_va, iln_va, AF.Ln, bias=cvals[:, 1:2]),
                    waits=[("ve", stats_ve)], inc="se")
                r_se = S.emit("se", lambda e: e.activation(
                    ilnr, iln_va, AF.Exp, scale=cvals[:, 2:3]), inc="se")
                s_ve = S.emit("ve", lambda e: e.tensor_tensor(
                    ilns, sc["mu"], ilnr, ALU.mult),
                    waits=[("se", r_se)], inc="ve")
                sc["prev"] = {"stats_ve": stats_ve, "red_te": iln_red_te,
                              "x2_te": x2_te_prev2}
                pend = {"s_ve": s_ve, "r_se": r_se}
            pend_reads, last_stream_gp = emit_iln_apply(pend, NG - 1)
            f32p.pop(); bf16p.pop()
            return S.val("ve")

        if upto >= 4:
            kvks = bf16p.t[:, KVKS_OFF:KVKS_OFF + NH // 2 * 130]
            p4_snap = p4_phase(kvks)
            if upto >= 5:
                last_ve = p5_phase(kvks, p4_snap)

        if upto >= 6:
            last_ve = ffn_phase(S.snap())

        f32p.push()
        S.fence(("te", "ve"), S.snap())
        ot = [f32p.alloc(D) for _ in range(2)]
        out_dma = {}
        pve = {}
        for t in range(NT):
            b = t % 2
            for half in range(2):
                bank = pb[2 * b + half]
                w = []
                if t >= 2 and half == 0:
                    w.append(out_dma[t - 2])

                def mk(tt=t, hh=half, bank=bank):
                    def f(e):
                        ins = None
                        src = stream_chunks_ap(tt * 128, 128, c0=hh * 4, cn=4)
                        for j in range(4):
                            ins = e.transpose(bank[:, j * 128:(j + 1) * 128],
                                              src[:, j, :], ident_f[:])
                        return ins
                    return f
                c_te = S.emit("te", mk(), waits=w, inc="te")
                pve[(t, half)] = S.emit(
                    "ve",
                    (lambda tt=t, bb=b, hh=half, bank=bank: lambda e:
                     e.tensor_copy(ot[bb][:, hh * 512:(hh + 1) * 512],
                                   bank[:, :]))(),
                    waits=[("te", c_te)], inc="ve")
            out_dma[t] = S.dma(
                "sy", y_out[t * 128:(t + 1) * 128, :], ot[b][:, :],
                waits=[("ve", pve[(t, 1)])])
        f32p.pop()
        S.wait("sy", [(f"sy{i}", S.val(f"sy{i}")) for i in range(DMA_R)]
               + [(f"gp{i}", S.val(f"gp{i}")) for i in range(DMA_R)])

        with nc.Block() as block:
            @block.sync
            def _(eng):
                for t in S.q["sy"]:
                    t(eng)

            @block.gpsimd
            def _(eng):
                for t in S.q["gp"]:
                    t(eng)

            @block.tensor
            def _(eng):
                for t in S.q["te"]:
                    t(eng)

            @block.vector
            def _(eng):
                for t in S.q["ve"]:
                    t(eng)

            @block.scalar
            def _(eng):
                for t in S.q["se"]:
                    t(eng)

    return nc


_PROG_CACHE = {}


def kernel(**inputs):
    upto = int(inputs.pop("_upto", 99))
    x = np.asarray(inputs["L0_attn_a_x"], np.float32)
    d = host_prep(inputs)
    if upto not in _PROG_CACHE:
        _PROG_CACHE[upto] = build_program(upto)
    nc = _PROG_CACHE[upto]
    shared = {
        "wgT": d["wgT"], "wuT": d["wuT"], "wdT": d["wdT"],
        "wqT": d["wqT"], "wkT": d["wkT"], "wvT": d["wvT"], "wddT": d["wddT"],
        "bwT": d["bwT"], "cwT": d["cwT"], "dD": d["dD"], "logA": d["logA"],
        "bq_p": d["bq_p"], "bd_p": d["bd_p"], "gamma_p": d["gamma_p"],
        "beta_p": d["beta_p"], "bk_r": d["bk_r"], "bv_r": d["bv_r"],
        "ident_in": d["ident"],
    }
    in_maps = []
    for c in range(8):
        m = dict(shared)
        m["x_in"] = np.ascontiguousarray(x[c])
        in_maps.append(m)
    res = run_bass_kernel_spmd(nc, in_maps, core_ids=list(range(8)))
    out = np.stack([res.results[c]["y_out"] for c in range(8)], axis=0)
    return out


# version 11
# speedup vs baseline: 1.0484x; 1.0484x over previous
import numpy as np
from contextlib import ExitStack

import concourse.bass as bass
import concourse.mybir as mybir
from concourse.bass_utils import run_bass_kernel_spmd

F32 = mybir.dt.float32
BF16 = mybir.dt.bfloat16
I32 = mybir.dt.int32
AF = mybir.ActivationFunctionType
ALU = mybir.AluOpType

L = 4096
DC = 8
D = 1024
H = 3072
HC = 24
NH = 16
HD = 64
TG = 512
NG = L // TG
NT = L // 128
QSCALE = 1.0 / 127.0
EPS_LN = 1e-5
EPS_ILN = 1e-12

ENGS = ("sy", "gp", "te", "ve", "se")
DMA_R = 4
SEM_NAMES = ["te", "ve", "se", "gpc"] + [f"sy{i}" for i in range(DMA_R)] + [
    f"gp{i}" for i in range(DMA_R)]

F32P_N = 35328
BF16P_N = 32768
AB_OFF = 34816
BWCW_OFF = 29696
P1ST_OFF = 23552
KVKS_OFF = 31728


class Pool:

    def __init__(self, tensor, ncols):
        self.t = tensor
        self.n = ncols
        self.off = 0
        self.marks = []

    def alloc(self, ncols):
        assert self.off + ncols <= self.n, f"pool overflow: {self.off}+{ncols}>{self.n}"
        ap = self.t[:, self.off:self.off + ncols]
        self.off += ncols
        return ap

    def push(self):
        self.marks.append(self.off)

    def pop(self):
        self.off = self.marks.pop()


class Sched:

    def __init__(self, nc, sems):
        self.nc = nc
        self.sems = sems
        self.cnt = {k: 0 for k in sems}
        self.q = {k: [] for k in ENGS}
        self.dma_idx = {"sy": 0, "gp": 0}

    def emit(self, eng, fn, waits=(), inc=None, inc_amt=1):
        waits = tuple((s, v) for s, v in waits if v > 0)
        for s, v in waits:
            assert v <= self.cnt[s], f"wait on future: {s} {v} > {self.cnt[s]}"
        if inc is not None:
            self.cnt[inc] += inc_amt
            post = self.cnt[inc]
            sem = self.sems[inc]
        else:
            post = None
            sem = None
        amt = inc_amt

        def thunk(e, fn=fn, waits=waits, sem=sem, amt=amt):
            for s, v in waits:
                e.wait_ge(self.sems[s], v)
            ins = fn(e)
            if sem is not None:
                assert ins is not None
                ins.then_inc(sem, amt)
        self.q[eng].append(thunk)
        return post

    def dma(self, eng, out_ap, in_ap, waits=()):
        i = self.dma_idx[eng]
        self.dma_idx[eng] = i + 1
        sem = f"{eng}{i % DMA_R}"
        w = [(sem, self.cnt[sem])] + list(waits)
        v = self.emit(eng, lambda e: e.dma_start(out_ap, in_ap),
                      waits=w, inc=sem, inc_amt=16)
        return (sem, v)

    def wait(self, eng, waits):
        for s, v in waits:
            assert v <= self.cnt[s]
        waits = tuple(waits)

        def thunk(e, waits=waits):
            for s, v in waits:
                e.wait_ge(self.sems[s], v)
        self.q[eng].append(thunk)

    def val(self, sem):
        return self.cnt[sem]

    def fence(self, writers, upstream):
        for w in writers:
            ws = [(s, v) for s, v in upstream if s != w and v > 0]
            if ws:
                self.wait(w, ws)

    def snap(self, engs=None):
        names = SEM_NAMES if engs is None else [
            s for s in SEM_NAMES if s[:2] in engs or s in engs]
        return [(s, self.cnt[s]) for s in names]


def host_prep(inputs):
    ca = np.ascontiguousarray
    d = {}
    d["wgT"] = ca(inputs["L0_ffn_w_gate"].T)
    d["wuT"] = ca(inputs["L0_ffn_w_up"].T)
    d["wdT"] = ca(inputs["L0_ffn_w_down"].T)
    d["wqT"] = ca(inputs["L0_attn_b_Wq"].T)
    d["wkT"] = ca(inputs["L0_attn_b_Wk"].T)
    d["wvT"] = ca(inputs["L0_attn_b_Wv"].T)
    d["wddT"] = ca(inputs["L0_attn_b_Wd"].T)
    d["bwT"] = ca(inputs["L0_attn_a_B_w"].T)
    d["cwT"] = ca(inputs["L0_attn_a_C_w"].T)
    Dp = np.asarray(inputs["L0_attn_a_D_param"], np.float32)
    dD = np.zeros((D, 128), np.float32)
    dD[np.arange(D), np.arange(D) % 128] = Dp
    d["dD"] = dD
    logA = np.asarray(inputs["L0_attn_a_log_A"], np.float32)
    d["logA"] = ca(np.broadcast_to(logA.reshape(128, 1), (128, TG)))

    def ppart(v):
        return ca(np.asarray(v, np.float32).reshape(DC, 128).T)
    d["bq_p"] = ppart(inputs["L0_attn_b_bq"])
    d["bd_p"] = ppart(inputs["L0_attn_b_bd"])
    d["gamma_p"] = ppart(inputs["L0_attn_b_gamma"])
    d["beta_p"] = ppart(inputs["L0_attn_b_beta"])
    d["bk_r"] = ca(np.asarray(inputs["L0_attn_b_bk"], np.float32).reshape(1, D))
    d["bv_r"] = ca(np.asarray(inputs["L0_attn_b_bv"], np.float32).reshape(1, D))
    d["ident"] = np.eye(128, dtype=np.float32)
    return d


P5LVL = 9

def build_program(upto=99):
    nc = bass.Bass("TRN2", target_bir_lowering=False)

    x_in = nc.dram_tensor("x_in", [L, D], F32, kind="ExternalInput")
    y_out = nc.dram_tensor("y_out", [L, D], F32, kind="ExternalOutput")
    wgT = nc.dram_tensor("wgT", [D, H], I32, kind="ExternalInput")
    wuT = nc.dram_tensor("wuT", [D, H], I32, kind="ExternalInput")
    wdT = nc.dram_tensor("wdT", [H, D], I32, kind="ExternalInput")
    wqT = nc.dram_tensor("wqT", [D, D], F32, kind="ExternalInput")
    wkT = nc.dram_tensor("wkT", [D, D], F32, kind="ExternalInput")
    wvT = nc.dram_tensor("wvT", [D, D], F32, kind="ExternalInput")
    wddT = nc.dram_tensor("wddT", [D, D], F32, kind="ExternalInput")
    bwT = nc.dram_tensor("bwT", [D, 128], F32, kind="ExternalInput")
    cwT = nc.dram_tensor("cwT", [128, D], F32, kind="ExternalInput")
    dD = nc.dram_tensor("dD", [D, 128], F32, kind="ExternalInput")
    logA = nc.dram_tensor("logA", [128, TG], F32, kind="ExternalInput")
    bq_p = nc.dram_tensor("bq_p", [128, DC], F32, kind="ExternalInput")
    bd_p = nc.dram_tensor("bd_p", [128, DC], F32, kind="ExternalInput")
    gamma_p = nc.dram_tensor("gamma_p", [128, DC], F32, kind="ExternalInput")
    beta_p = nc.dram_tensor("beta_p", [128, DC], F32, kind="ExternalInput")
    bk_r = nc.dram_tensor("bk_r", [1, D], F32, kind="ExternalInput")
    bv_r = nc.dram_tensor("bv_r", [1, D], F32, kind="ExternalInput")
    ident_in = nc.dram_tensor("ident_in", [128, 128], F32, kind="ExternalInput")
    sg = nc.dram_tensor("sg", [HC, 128, DC * 128], BF16, kind="Internal")
    su = nc.dram_tensor("su", [HC, 128, DC * 128], BF16, kind="Internal")
    sd = nc.dram_tensor("sd", [DC, 128, HC * 128], BF16, kind="Internal")
    sq = nc.dram_tensor("sq", [DC, 128, DC * 128], BF16, kind="Internal")
    sdd = nc.dram_tensor("sdd", [DC, 128, DC * 128], BF16, kind="Internal")

    st = ExitStack()
    with st:
        f32p = Pool(st.enter_context(nc.sbuf_tensor("f32p", [128, F32P_N], F32)),
                    AB_OFF)
        bf16p = Pool(st.enter_context(nc.sbuf_tensor("bf16p", [128, BF16P_N], BF16)),
                     BF16P_N)
        stream = f32p.alloc(DC * L)
        A_b = f32p.t[:, AB_OFF:AB_OFF + TG]
        bwcw = bf16p.t[:, BWCW_OFF:BWCW_OFF + 3 * D]
        bw_b = bwcw[:, 0:D]
        cw_b = bwcw[:, D:2 * D]
        dD_b = bwcw[:, 2 * D:3 * D]
        p1st = bf16p.t[:, P1ST_OFF:P1ST_OFF + 6144]

        ident_f = st.enter_context(nc.sbuf_tensor("ident_f", [128, 128], F32))
        ident_b = st.enter_context(nc.sbuf_tensor("ident_b", [128, 128], BF16))
        ones_b = st.enter_context(nc.sbuf_tensor("ones_b", [128, 128], BF16))
        onescol_b = st.enter_context(nc.sbuf_tensor("onescol_b", [128, 1], BF16))
        bk_b = st.enter_context(nc.sbuf_tensor("bk_b", [1, D], BF16))
        bv_b = st.enter_context(nc.sbuf_tensor("bv_b", [1, D], BF16))
        cvals = st.enter_context(nc.sbuf_tensor("cvals", [128, 32], F32))
        bq_s = st.enter_context(nc.sbuf_tensor("bq_s", [128, DC], F32))
        bd_s = st.enter_context(nc.sbuf_tensor("bd_s", [128, DC], F32))
        gam_s = st.enter_context(nc.sbuf_tensor("gam_s", [128, DC], F32))
        bet_s = st.enter_context(nc.sbuf_tensor("bet_s", [128, DC], F32))
        carry = cvals[:, 4:5]

        pb = [st.enter_context(nc.psum_tensor(f"pb{i}", [128, 512], F32))
              for i in range(8)]

        sems = {k: st.enter_context(nc.semaphore(f"sem_{k}")) for k in SEM_NAMES}
        S = Sched(nc, sems)

        def stream_ap(c, l0, n):
            return stream[:, c * L + l0: c * L + l0 + n]

        def stream_chunks_ap(l0, n, c0=0, cn=DC):
            r = stream.rearrange("p (c l) -> p c l", c=DC)
            return r[:, c0:c0 + cn, l0:l0 + n]

        S.emit("ve", lambda e: e.memset(ones_b[:], 1.0), inc="ve")
        S.emit("ve", lambda e: e.memset(cvals[:, 0:1], EPS_LN), inc="ve")
        S.emit("ve", lambda e: e.memset(cvals[:, 1:2], EPS_ILN), inc="ve")
        S.emit("ve", lambda e: e.memset(cvals[:, 2:3], -0.5), inc="ve")
        S.emit("ve", lambda e: e.memset(cvals[:, 5:6], -1.0), inc="ve")
        S.emit("ve", lambda e: e.memset(cvals[:, 6:7], QSCALE), inc="ve")
        neg1 = cvals[:, 5:6]
        qsc = cvals[:, 6:7]
        S.emit("ve", lambda e: e.memset(onescol_b[:], 1.0), inc="ve")
        c_id = S.dma("sy", ident_f[:], ident_in[:, :])
        S.dma("sy", bq_s[:], bq_p[:, :])
        S.dma("sy", bd_s[:], bd_p[:, :])
        S.dma("sy", gam_s[:], gamma_p[:, :])
        S.dma("sy", bet_s[:], beta_p[:, :])
        c_la = S.dma("sy", A_b, logA[:, :])
        se_A = S.emit("se", lambda e: e.activation(A_b, A_b, AF.Exp),
                      waits=[c_la], inc="se")
        cst_se = S.emit("se", lambda e: e.activation(ident_b[:], ident_f[:], AF.Copy),
                        waits=[c_id], inc="se")
        S.dma("gp", bk_b[0:1, :], bk_r[:, :])
        S.dma("gp", bv_b[0:1, :], bv_r[:, :])
        S.dma("gp", bw_b[:, :].rearrange("p (k n) -> p k n", k=DC),
              bwT[:, :].rearrange("(k p) n -> p k n", p=128))
        S.dma("gp", cw_b[:, :], cwT[:, :])
        attw_gp = S.dma("gp", dD_b[:, :].rearrange("p (k n) -> p k n", k=DC),
                        dD[:, :].rearrange("(k p) n -> p k n", p=128))

        p1_ready = {"sg": [None] * HC, "su": [None] * HC, "sd": [None] * DC,
                    "sq": [None] * DC, "sdd": [None] * DC}
        stage = [p1st[:, 0:3072], p1st[:, 3072:6144]]
        prev_out = [None, None]
        bi = 0

        def p1_piece(src_t, dst, name, kchunks, cbatch, ksplit):
            nonlocal bi, prev_out
            cols = 128 * cbatch
            nch = dst.shape[0]
            src3 = src_t.rearrange("(k p) n -> p k n", p=128)
            for cb in range(nch // cbatch):
                b2 = bi % 2
                bi += 1
                stg = stage[b2][:, 0:kchunks * cols]
                stg3 = stg.rearrange("p (k n) -> p k n", n=cols)
                kq = kchunks // ksplit
                c_in = None
                for s2 in range(ksplit):
                    w = []
                    if s2 == 0 and prev_out[b2] is not None:
                        w.append(prev_out[b2])
                    c_in = S.dma(
                        "gp", stg3[:, s2 * kq:(s2 + 1) * kq, :],
                        src3[:, s2 * kq:(s2 + 1) * kq,
                             cb * cols:(cb + 1) * cols], waits=w)
                for j in range(cbatch):
                    c = cb * cbatch + j
                    o = S.dma("gp", dst[c, :, :],
                              stg3[:, :, j * 128:(j + 1) * 128],
                              waits=[c_in] if j == 0 else ())
                    p1_ready[name][c] = o
                prev_out[b2] = p1_ready[name][cb * cbatch + cbatch - 1]

        if upto >= 3:
            p1_piece(wgT, sg, "sg", DC, 3, 1)
            p1_piece(wuT, su, "su", DC, 3, 1)
            p1_piece(wdT, sd, "sd", HC, 1, 3)
        if upto >= 5:
            p1_piece(wqT, sq, "sq", DC, 2, 1)
            p1_piece(wddT, sdd, "sdd", DC, 2, 1)
        p1_se = S.val("se")

        f32p.push()
        xt = [f32p.alloc(D) for _ in range(2)]
        p0_ve = {}
        for t in range(NT):
            b = t % 2
            w = [("ve", p0_ve[t - 2])] if t >= 2 else []
            c_in = S.dma("sy", xt[b][:, :], x_in[t * 128:(t + 1) * 128, :], waits=w)
            tw = [c_in, ("se", cst_se)] + (
                [("ve", p0_ve[t - 2])] if t >= 2 else [])
            for half in range(2):
                bank = pb[2 * b + half]

                def mk(tt=t, bb=b, hh=half, bank=bank):
                    def f(e):
                        ins = None
                        for j in range(4):
                            c = hh * 4 + j
                            ins = e.transpose(bank[:, j * 128:(j + 1) * 128],
                                              xt[bb][:, c * 128:(c + 1) * 128],
                                              ident_f[:])
                        return ins
                    return f
                c_te = S.emit("te", mk(), waits=tw if half == 0 else (), inc="te")

                def mkev(tt=t, hh=half, bank=bank):
                    def f(e):
                        dst = stream_chunks_ap(tt * 128, 128, c0=hh * 4, cn=4)
                        src = bank[:, :].rearrange("p (c l) -> p c l", c=4)
                        return e.tensor_copy(dst, src)
                    return f
                p0_ve[t] = S.emit("ve", mkev(), waits=[("te", c_te)], inc="ve")
        p0_snap = S.snap(("sy", "te", "ve"))
        p0_done_ve = S.val("ve")
        f32p.pop()

        def ln_stats(g, xsrc_chunks, sc, eps, red_banks, pre_w_se=(), pre_w_te=()):
            bA, bB = pb[red_banks[0]], pb[red_banks[1]]
            prev = sc.get("prev", {})
            se_c = ve_c = te_c = None
            x2_te_prev = prev.get("x2_te")
            tex_prev = prev.get("red_te")
            sq_prev = None
            for c in range(DC):
                w_se = list(pre_w_se) if c == 0 else []
                if tex_prev is not None:
                    w_se.append(("te", tex_prev))
                if c == 0 and "apply_ve" in prev:
                    w_se.append(("ve", prev["apply_ve"]))
                se_c = S.emit(
                    "se",
                    (lambda cc: lambda e: e.activation(
                        sc["x_bf"], xsrc_chunks(cc), AF.Copy))(c),
                    waits=w_se, inc="se")
                w_ve = list(pre_w_se) if c == 0 else []
                if x2_te_prev is not None:
                    w_ve.append(("te", x2_te_prev))
                sq_prev = ve_c = S.emit(
                    "ve",
                    (lambda cc: lambda e: e.tensor_tensor(
                        sc["x2"], xsrc_chunks(cc), xsrc_chunks(cc),
                        ALU.mult))(c),
                    waits=w_ve, inc="ve")
                w = [("se", se_c)]
                if c == 0:
                    w += list(pre_w_te)
                    if "stats_ve" in prev:
                        w.append(("ve", prev["stats_ve"]))
                tex_prev = S.emit(
                    "te",
                    (lambda cc: lambda e: e.matmul(
                        bA[:, :], ones_b[:], sc["x_bf"],
                        start=(cc == 0), stop=(cc == DC - 1)))(c),
                    waits=w, inc="te")
                te_c = S.emit(
                    "te",
                    (lambda cc: lambda e: e.matmul(
                        bB[:, :], ones_b[:], sc["x2"],
                        start=(cc == 0), stop=(cc == DC - 1)))(c),
                    waits=[("ve", ve_c)], inc="te")
                x2_te_prev = te_c
            red_te = te_c
            ve_c = S.emit("ve", lambda e: e.tensor_scalar(
                sc["mu"], bA[:, 0:TG], 1.0 / D, None, ALU.mult),
                waits=[("te", red_te)], inc="ve")
            ve_c = S.emit("ve", lambda e: e.tensor_tensor(
                sc["s2"], sc["mu"], sc["mu"], ALU.mult), inc="ve")
            stats_ve = S.emit("ve", lambda e: e.scalar_tensor_tensor(
                sc["va"], bB[:, 0:TG], 1.0 / D, sc["s2"], ALU.mult, ALU.subtract),
                waits=[("te", red_te)], inc="ve")
            eps_ap = cvals[:, 0:1] if eps == EPS_LN else cvals[:, 1:2]
            se_c = S.emit("se", lambda e: e.activation(
                sc["va"], sc["va"], AF.Ln, bias=eps_ap),
                waits=[("ve", stats_ve)], inc="se")
            r_se = S.emit("se", lambda e: e.activation(
                sc["r_bf"], sc["va"], AF.Exp, scale=cvals[:, 2:3]), inc="se")
            s_ve = S.emit("ve", lambda e: e.tensor_tensor(
                sc["s_bf"], sc["mu"], sc["r_bf"], ALU.mult),
                waits=[("se", r_se)], inc="ve")
            sc["prev"] = {"red_te": red_te, "x2_te": x2_te_prev,
                          "stats_ve": stats_ve}
            return {"r_se": r_se, "s_ve": s_ve, "red_te": red_te,
                    "stats_ve": stats_ve}

        def ln_apply(st_r, xsrc_chunks, xn_dst, sc, apply_w=()):
            ve_c = None
            for c in range(DC):
                w_se = [("ve", st_r["s_ve"])] if c == 0 else [("ve", ve_c)]
                last_cast = S.emit(
                    "se",
                    (lambda cc: lambda e: e.activation(
                        sc["xa_bf"], xsrc_chunks(cc), AF.Copy))(c),
                    waits=w_se, inc="se")
                S.emit("ve",
                       (lambda cc: lambda e: e.tensor_tensor(
                           sc["t_bf"], sc["xa_bf"], sc["r_bf"],
                           ALU.mult))(c),
                       waits=[("se", last_cast)] + (list(apply_w) if c == 0 else []),
                       inc="ve")
                ve_c = S.emit("ve",
                              (lambda cc: lambda e: e.tensor_tensor(
                                  xn_dst(cc), sc["t_bf"], sc["s_bf"],
                                  ALU.subtract))(c),
                              inc="ve")
            sc["prev"]["apply_ve"] = ve_c
            return ve_c

        def ln_group(g, xsrc_chunks, xn_dst, sc, eps, red_banks,
                     pre_w_se=(), pre_w_te=(), apply_w=()):
            st_r = ln_stats(g, xsrc_chunks, sc, eps, red_banks, pre_w_se, pre_w_te)
            ve_c = ln_apply(st_r, xsrc_chunks, xn_dst, sc, apply_w)
            return {"te": st_r["red_te"], "apply_ve": ve_c, "se": st_r["r_se"],
                    "stats_ve": st_r["stats_ve"]}

        def ln_scratch(pool_b, pool_f):
            return {
                "x_bf": pool_b.alloc(TG),
                "xa_bf": pool_b.alloc(TG),
                "x2": pool_b.alloc(TG),
                "t_bf": pool_b.alloc(TG),
                "r_bf": pool_b.alloc(TG),
                "s_bf": pool_b.alloc(TG),
                "mu": pool_f.alloc(TG),
                "va": pool_f.alloc(TG),
                "s2": pool_f.alloc(TG),
            }

        last_ve = p0_done_ve
        if upto >= 2:
            f32p.push(); bf16p.push()
            S.fence(("sy", "te", "ve", "se"), p0_snap)
            sc_a = ln_scratch(bf16p, f32p)
            sc_b = {
                "x_bf": bf16p.alloc(TG),
                "xa_bf": bf16p.alloc(TG),
                "x2": bf16p.alloc(TG),
                "t_bf": bf16p.alloc(TG),
                "r_bf": bf16p.alloc(TG),
                "s_bf": bf16p.alloc(TG),
                "mu": sc_a["mu"], "s2": sc_a["s2"],
                "va": f32p.alloc(TG),
            }
            sc2 = [sc_a, sc_b]
            xn1 = bf16p.alloc(DC * (L // 2))
            hs_g = [bf16p.alloc(TG) for _ in range(2)]

            def xn1_ap(c, g, n=TG):
                gg = g % 4
                return xn1[:, c * (L // 2) + gg * TG: c * (L // 2) + gg * TG + n]

            scan_prev = None
            yev_ve = {}
            ylast = {}
            y_te_last = {}
            yend_ve = yend_te = None
            for h in range(2):
                xp_te = {}
                scan_ve = {}
                for g4 in range(4):
                    g = h * 4 + g4
                    pre_te = [("ve", yend_ve)] if (h > 0 and g4 <= 1) else []
                    app_w = [("te", yend_te)] if (h > 0 and g4 == 0) else []
                    r = ln_group(
                        g, lambda c, gg=g: stream_ap(c, gg * TG, TG),
                        lambda c, gg=g: xn1_ap(c, gg), sc2[g % 2], EPS_LN,
                        red_banks=(4, 5) if g % 2 == 0 else (6, 7),
                        pre_w_te=pre_te, apply_w=app_w)
                    for k in range(DC):
                        w = []
                        if k == 0:
                            w = [("ve", r["apply_ve"]), attw_gp]
                            if h > 0:
                                w.append(("ve", scan_prev[g4]))
                        xp_te[g4] = S.emit(
                            "te",
                            (lambda kk, gg4=g4, gg=g: lambda e: e.matmul(
                                pb[gg4][:, :], bw_b[:, kk * 128:(kk + 1) * 128],
                                xn1_ap(kk, gg),
                                start=(kk == 0), stop=(kk == DC - 1)))(k),
                            waits=w, inc="te" if k == DC - 1 else None)
                for g4 in range(4):
                    g = h * 4 + g4
                    if g == 0:
                        init = 0.0
                    else:
                        init = hs_g[(g4 - 1) % 2][:, TG - 1:TG]
                    w = [("se", se_A), ("te", xp_te[g4])]
                    if g - 2 in y_te_last:
                        w.append(("te", y_te_last[g - 2]))
                    scan_ve[g4] = S.emit(
                        "ve",
                        (lambda gg4=g4, init=init: lambda e: e.tensor_tensor_scan(
                            hs_g[gg4 % 2][:, :], A_b, pb[gg4][:, :], init,
                            ALU.mult, ALU.add))(g4),
                        waits=w, inc="ve")
                    for c in range(DC):
                        bank = pb[4 + (c % 4)]
                        w = [("ve", scan_ve[g4])] if c == 0 else []
                        key = (g, c)
                        prev = (g, c - 4) if c >= 4 else (
                            (g - 1, c + 4) if g > 0 else None)
                        if prev in yev_ve:
                            w.append(("ve", yev_ve[prev]))
                        S.emit("te",
                               (lambda cc, bank=bank, gg4=g4: lambda e: e.matmul(
                                   bank[:, :], cw_b[:, cc * 128:(cc + 1) * 128],
                                   hs_g[gg4 % 2][:, :], start=True, stop=False))(c),
                               waits=w, inc=None)
                        te_c = S.emit(
                            "te",
                            (lambda cc, gg=g, bank=bank: lambda e: e.matmul(
                                bank[:, :], dD_b[:, cc * 128:(cc + 1) * 128],
                                xn1_ap(cc, gg), start=False, stop=True))(c),
                            inc="te")
                        y_te_last[g] = te_c
                        yev_ve[key] = S.emit(
                            "ve",
                            (lambda cc, gg=g, bank=bank: lambda e: e.tensor_tensor(
                                stream_ap(cc, gg * TG, TG), bank[:, :],
                                stream_ap(cc, gg * TG, TG), ALU.add))(c),
                            waits=[("te", te_c)], inc="ve")
                scan_prev = scan_ve
                yend_ve = S.val("ve")
                yend_te = S.val("te")
            last_ve = S.val("ve")
            f32p.pop(); bf16p.pop()

        def ffn_phase(upstream):
            f32p.push(); bf16p.push()
            S.fence(("sy", "te", "ve", "se"), upstream)
            sc = ln_scratch(bf16p, f32p)
            xn2 = [bf16p.alloc(DC * TG) for _ in range(2)]
            hbuf = bf16p.alloc(HC * TG)
            sgt = bf16p.alloc(TG)
            wg_buf = [bf16p.alloc(DC * 128) for _ in range(2)]
            wu_buf = [bf16p.alloc(DC * 128) for _ in range(2)]
            wd_buf = [bf16p.alloc(HC * 64) for _ in range(2)]

            gu_te = {}
            gu_dma = {}
            silu_se = {}
            mult_ve = {}
            dn_ev = {}
            dn_te = {}
            dn_half_te = {}
            apply_done = {}

            def emit_ln(g):
                aw = []
                if ("u", HC - 1, g - 2) in gu_te:
                    aw.append(("te", gu_te[("u", HC - 1, g - 2)]))
                r = ln_group(
                    g, lambda c, gg=g: stream_ap(c, gg * TG, TG),
                    lambda c, gg=g: xn2[gg % 2][:, c * TG:(c + 1) * TG],
                    sc, EPS_LN, red_banks=(6, 7), apply_w=aw)
                apply_done[g] = r["apply_ve"]

            emit_ln(0)
            for g in range(NG):
                if g + 1 < NG:
                    emit_ln(g + 1)
                xn = xn2[g % 2]
                for c in range(HC):
                    b = c % 2
                    w = [p1_ready["sg"][c]]
                    if ("g", c - 2, g) in gu_te:
                        w.append(("te", gu_te[("g", c - 2, g)]))
                    elif g > 0 and c < 2:
                        w.append(("te", gu_te[("g", HC - 2 + c, g - 1)]))
                    gu_dma[("g", c)] = S.dma("sy", wg_buf[b][:, :], sg[c, :, :],
                                             waits=w)
                    w = [p1_ready["su"][c]]
                    if ("u", c - 2, g) in gu_te:
                        w.append(("te", gu_te[("u", c - 2, g)]))
                    elif g > 0 and c < 2:
                        w.append(("te", gu_te[("u", HC - 2 + c, g - 1)]))
                    gu_dma[("u", c)] = S.dma("sy", wu_buf[b][:, :], su[c, :, :],
                                             waits=w)
                    for k in range(DC):
                        w = []
                        if k == 0:
                            w = [gu_dma[("g", c)], ("ve", apply_done[g])]
                            if c >= 2:
                                w.append(("se", silu_se[c - 2]))
                            elif g > 0:
                                w.append(("se", silu_se[HC - 2 + c]))
                        gu_te[("g", c, g)] = S.emit(
                            "te",
                            (lambda kk, cc=c, bb=b, xn=xn: lambda e: e.matmul(
                                pb[bb][:, :], wg_buf[bb][:, kk * 128:(kk + 1) * 128],
                                xn[:, kk * TG:(kk + 1) * TG],
                                start=(kk == 0), stop=(kk == DC - 1)))(k),
                            waits=w, inc="te" if k == DC - 1 else None)
                    for k in range(DC):
                        w = []
                        if k == 0:
                            w = [gu_dma[("u", c)]]
                            if c >= 2:
                                w.append(("ve", mult_ve[c - 2]))
                            elif g > 0:
                                w.append(("ve", mult_ve[HC - 2 + c]))
                        gu_te[("u", c, g)] = S.emit(
                            "te",
                            (lambda kk, cc=c, bb=b, xn=xn: lambda e: e.matmul(
                                pb[2 + bb][:, :], wu_buf[bb][:, kk * 128:(kk + 1) * 128],
                                xn[:, kk * TG:(kk + 1) * TG],
                                start=(kk == 0), stop=(kk == DC - 1)))(k),
                            waits=w, inc="te" if k == DC - 1 else None)
                    w = [("te", gu_te[("g", c, g)])]
                    if c >= 1:
                        w.append(("ve", mult_ve[c - 1]))
                    elif g > 0:
                        w.append(("ve", mult_ve[HC - 1]))
                    silu_se[c] = S.emit(
                        "se",
                        (lambda bb=b: lambda e: e.activation(
                            sgt, pb[bb][:, :], AF.Silu, scale=qsc))(),
                        waits=w, inc="se")
                    mult_ve[c] = S.emit(
                        "ve",
                        (lambda cc=c, bb=b: lambda e: e.tensor_tensor(
                            hbuf[:, cc * TG:(cc + 1) * TG], pb[2 + bb][:, :],
                            sgt, ALU.mult))(c),
                        waits=[("te", gu_te[("u", c, g)]), ("se", silu_se[c])],
                        inc="ve")
                HH = HC // 2
                for cd in range(DC):
                    for hf in range(2):
                        hidx = 2 * cd + hf
                        bb2 = hidx % 2
                        w = [p1_ready["sd"][cd]]
                        if (g, hidx) != (0, 0):
                            key = (hidx - 2) % 16
                            if key in dn_half_te:
                                w.append(("te", dn_half_te[key]))
                        c_dma = S.dma(
                            "sy", wd_buf[bb2][:, :],
                            sd[cd, :, :][:, hf * HH * 128:(hf + 1) * HH * 128],
                            waits=w)
                        for k2 in range(HH):
                            k = hf * HH + k2
                            w = []
                            if k2 == 0:
                                w = [c_dma]
                                if hf == 0:
                                    w.append(("ve", mult_ve[HC - 1]))
                                    if cd >= 2:
                                        w.append(("ve", dn_ev[cd - 2]))
                                    elif g > 0:
                                        w.append(("ve", dn_ev[DC - 2 + cd]))
                            ins = S.emit(
                                "te",
                                (lambda kk, k2=k2, bb2=bb2, cc=cd: lambda e: e.matmul(
                                    pb[4 + (cc % 2)][:, :],
                                    wd_buf[bb2][:, k2 * 128:(k2 + 1) * 128],
                                    hbuf[:, kk * TG:(kk + 1) * TG],
                                    start=(kk == 0), stop=(kk == HC - 1)))(k),
                                waits=w,
                                inc="te" if k2 == HH - 1 else None)
                            if ins is not None:
                                dn_half_te[hidx % 16] = ins
                                dn_te[cd] = ins
                    dn_ev[cd] = S.emit(
                        "ve",
                        (lambda cc=cd, gg=g: lambda e: e.scalar_tensor_tensor(
                            stream_ap(cc, gg * TG, TG), pb[4 + (cc % 2)][:, :],
                            QSCALE * QSCALE, stream_ap(cc, gg * TG, TG),
                            ALU.mult, ALU.add))(cd),
                        waits=[("te", dn_te[cd])], inc="ve")
            out = S.val("ve")
            f32p.pop(); bf16p.pop()
            return out

        if upto >= 3:
            last_ve = ffn_phase(S.snap())

        def p4_phase(kvks):
            f32p.push(); bf16p.push()
            S.fence(("sy", "gp", "te", "ve", "se"), S.snap())
            sc = ln_scratch(bf16p, f32p)
            xn3db = [bf16p.alloc(DC * TG) for _ in range(2)]
            wk_bf = bf16p.alloc(DC * D)
            wv_bf = bf16p.alloc(DC * D)
            k1_bf = bf16p.alloc(D)
            v1_bf = bf16p.alloc(D)
            t1_bf = bf16p.alloc(TG)
            e_bf = bf16p.alloc(TG)

            wkv_gp = None
            for k in range(DC):
                S.dma("gp", wk_bf[:, k * D:(k + 1) * D],
                      wkT[k * 128:(k + 1) * 128, :])
                wkv_gp = S.dma("gp", wv_bf[:, k * D:(k + 1) * D],
                               wvT[k * 128:(k + 1) * 128, :])
            ksum_acc = f32p.alloc(8)
            S.emit("ve", lambda e: e.memset(kvks[:, :], 0.0), inc="ve")
            S.emit("ve", lambda e: e.memset(ksum_acc, 0.0), inc="ve")
            kv_mm_last = None
            ksum_mm_last = None
            elu_ve_prev = None
            vev_se_prev = None
            vev0 = vev1 = None
            ksacc_ve = None
            apply_done = {}
            last_proj_te = {}

            def emit_ln4(g):
                aw = []
                if g - 2 in last_proj_te:
                    aw.append(("te", last_proj_te[g - 2]))
                r = ln_group(
                    g, lambda c, gg=g: stream_ap(c, gg * TG, TG),
                    lambda c, gg=g: xn3db[gg % 2][:, c * TG:(c + 1) * TG],
                    sc, EPS_LN, red_banks=(6, 7), apply_w=aw)
                apply_done[g] = r["apply_ve"]

            emit_ln4(0)
            for g in range(NG):
                if g + 1 < NG:
                    emit_ln4(g + 1)
                xn3 = xn3db[g % 2]
                r = {"apply_ve": apply_done[g]}
                for t in range(4):
                    T = g * 4 + t
                    for half in range(2):
                        for k in range(DC):
                            w = []
                            if half == 0 and k == 0:
                                w = [("ve", r["apply_ve"]), wkv_gp]
                                if elu_ve_prev is not None:
                                    w.append(("ve", elu_ve_prev))
                                if ksacc_ve is not None:
                                    w.append(("ve", ksacc_ve))
                            S.emit(
                                "te",
                                (lambda kk, hh=half, gg=g, tt=t, xn3=xn3: lambda e: e.matmul(
                                    pb[hh][:, :],
                                    xn3[:, kk * TG + tt * 128: kk * TG + (tt + 1) * 128],
                                    wk_bf[:, kk * D + hh * 512: kk * D + (hh + 1) * 512],
                                    start=(kk == 0), stop=False))(k),
                                waits=w, inc=None)
                        kp_te = S.emit(
                            "te",
                            (lambda hh=half: lambda e: e.matmul(
                                pb[hh][:, :], ones_b[0:1, 0:128],
                                bk_b[0:1, hh * 512:(hh + 1) * 512],
                                start=False, stop=True))(),
                            inc="te")
                        w = [("te", kp_te)]
                        if elu_ve_prev is not None:
                            w.append(("ve", elu_ve_prev))
                        if half == 0 and ksum_mm_last is not None:
                            w.append(("te", ksum_mm_last))
                        se1 = S.emit(
                            "se",
                            (lambda hh=half: lambda e: e.activation(
                                t1_bf, pb[hh][:, :], AF.Relu, scale=neg1))(),
                            waits=w, inc="se")
                        se2 = S.emit(
                            "se",
                            lambda e: e.activation(e_bf, t1_bf, AF.Exp, scale=neg1),
                            inc="se")
                        elu_ve_prev = S.emit(
                            "ve",
                            (lambda hh=half: lambda e: e.scalar_tensor_tensor(
                                k1_bf[:, hh * 512:(hh + 1) * 512], pb[hh][:, :],
                                0.0, e_bf, ALU.max, ALU.add))(),
                            waits=[("se", se2)], inc="ve")
                    for half in range(2):
                        for k in range(DC):
                            w = []
                            if k == 0 and half == 0 and vev0 is not None:
                                w = [vev0, vev1]
                            S.emit(
                                "te",
                                (lambda kk, hh=half, gg=g, tt=t, xn3=xn3: lambda e: e.matmul(
                                    pb[2 + hh][:, :],
                                    xn3[:, kk * TG + tt * 128: kk * TG + (tt + 1) * 128],
                                    wv_bf[:, kk * D + hh * 512: kk * D + (hh + 1) * 512],
                                    start=(kk == 0), stop=False))(k),
                                waits=w, inc=None)
                        vp_te = S.emit(
                            "te",
                            (lambda hh=half: lambda e: e.matmul(
                                pb[2 + hh][:, :], ones_b[0:1, 0:128],
                                bv_b[0:1, hh * 512:(hh + 1) * 512],
                                start=False, stop=True))(),
                            inc="te")
                        last_proj_te[g] = vp_te
                        w = [("te", vp_te)]
                        if half == 0 and kv_mm_last is not None:
                            w.append(("te", kv_mm_last))
                        if half == 0:
                            vev_se_prev = S.emit(
                                "se",
                                (lambda hh=half: lambda e: e.activation(
                                    v1_bf[:, hh * 512:(hh + 1) * 512],
                                    pb[2 + hh][:, :], AF.Copy))(),
                                waits=w, inc="se")
                            vev0 = ("se", vev_se_prev)
                        else:
                            vev1 = ("ve", S.emit(
                                "ve",
                                (lambda hh=half: lambda e: e.tensor_copy(
                                    v1_bf[:, hh * 512:(hh + 1) * 512],
                                    pb[2 + hh][:, :]))(),
                                waits=w, inc="ve"))
                    w0 = [vev0, vev1, ("ve", elu_ve_prev)]
                    for p in range(NH // 2):
                        bank = pb[4 + p // 4]
                        base = (p % 4) * 128
                        for odd in range(2):
                            hh = 2 * p + odd
                            tp = (0, 64 * odd)
                            po = 64 * odd
                            lhs = k1_bf[:, hh * 64:(hh + 1) * 64]
                            w = w0 if (p == 0 and odd == 0) else ()
                            last_pair = (p == NH // 2 - 1 and odd == 1)
                            kv_mm_last = S.emit(
                                "te",
                                (lambda lhs=lhs, bank=bank, base=base, po=po, tp=tp,
                                 hh=hh, TT=T: lambda e: e.matmul(
                                     bank[po:po + 64, base + po:base + po + 64], lhs,
                                     v1_bf[:, hh * 64:(hh + 1) * 64],
                                     start=(TT == 0), stop=(TT == NT - 1),
                                     tile_position=tp, skip_group_check=True))(),
                                waits=w,
                                inc="te" if last_pair else None) or kv_mm_last
                    for p in range(NH // 2):
                        for odd in range(2):
                            hh = 2 * p + odd
                            tp = (0, 64 * odd)
                            po = 64 * odd
                            w = []
                            if p == 0 and odd == 0:
                                w = [("ve", elu_ve_prev)]
                            last_pair = (p == NH // 2 - 1 and odd == 1)
                            ksum_mm_last = S.emit(
                                "te",
                                (lambda hh=hh, po=po, tp=tp, p=p: lambda e: e.matmul(
                                    pb[0][po:po + 64, 496 + p:497 + p],
                                    k1_bf[:, hh * 64:(hh + 1) * 64], onescol_b[:, :],
                                    start=True, stop=True,
                                    tile_position=tp, skip_group_check=True))(),
                                waits=w,
                                inc="te" if last_pair else None) or ksum_mm_last
                    wks = [("te", ksum_mm_last)]
                    ksacc_ve = S.emit(
                        "ve",
                        lambda e: e.tensor_tensor(ksum_acc, pb[0][:, 496:504],
                                                  ksum_acc, ALU.add),
                        waits=wks, inc="ve")
            kv_ev = None
            for p in range(NH // 2):
                bank = pb[4 + p // 4]
                base = (p % 4) * 128
                for odd in range(2):
                    po = 64 * odd
                    w = [("te", kv_mm_last)] if kv_ev is None else []
                    S.emit(
                        "ve",
                        (lambda bank=bank, base=base, po=po, p=p: lambda e: e.tensor_copy(
                            kvks[po:po + 64, p * 130 + po: p * 130 + po + 64],
                            bank[po:po + 64, base + po:base + po + 64]))(),
                        waits=w, inc="ve")
                    kv_ev = S.emit(
                        "ve",
                        (lambda po=po, p=p, odd=odd: lambda e: e.tensor_scalar(
                            kvks[po:po + 64, p * 130 + 128 + odd: p * 130 + 129 + odd],
                            ksum_acc[po:po + 64, p:p + 1],
                            1e-6, None, ALU.add))(),
                        waits=[("ve", ksacc_ve)] if kv_ev is None else [],
                        inc="ve")
            f32p.pop(); bf16p.pop()
            return S.snap()

        def p5_phase(kvks, p4_snap):
            f32p.push(); bf16p.push()
            S.fence(("sy", "gp", "te", "ve", "se"), p4_snap)
            sc = ln_scratch(bf16p, f32p)
            xn3db = [bf16p.alloc(DC * TG) for _ in range(2)]
            q1 = bf16p.alloc(DC * TG)
            hs_bf = q1
            t1_bf = bf16p.alloc(TG)
            t2_bf = bf16p.alloc(TG)
            e_bf = bf16p.alloc(TG)
            ctx_bf = [bf16p.alloc(D) for _ in range(2)]
            ctx_fm = bf16p.alloc(DC * TG)
            qw_buf = [bf16p.alloc(DC * 128) for _ in range(2)]
            dw_buf = [bf16p.alloc(DC * 128) for _ in range(2)]
            z_f = cvals[:, 8:24]
            neg_bq = cvals[:, 24:32]
            ilnr = bf16p.alloc(TG)
            ilns = bf16p.alloc(TG)
            iln_va = f32p.alloc(TG)
            ta = [bf16p.alloc(TG) for _ in range(2)]
            tb = [bf16p.alloc(TG) for _ in range(2)]
            tc = ta
            assert f32p.off <= AB_OFF and bf16p.off <= KVKS_OFF, (
                f"P5 pools overlap tails: {f32p.off} {bf16p.off}")

            S.emit("ve", lambda e: e.tensor_scalar(
                neg_bq, bq_s[:, :], -1.0, None, ALU.mult), inc="ve")

            q_dma = {}
            q_te = {}
            q_ve = {}
            q_se3 = {}
            ctx_ts_ve = None
            tp_ve_prev = None
            hs_te = {}
            hs_ev = {}
            iln_red_te = None
            iln_apply_ve = None
            stream_ve_prev = None
            apply_done5 = {}
            q_last_te = {}

            def emit_ln5(g):
                aw = []
                if g - 2 in q_last_te:
                    aw.append(("te", q_last_te[g - 2]))
                r = ln_group(
                    g, lambda c, gg=g: stream_ap(c, gg * TG, TG),
                    lambda c, gg=g: xn3db[gg % 2][:, c * TG:(c + 1) * TG],
                    sc, EPS_LN, red_banks=(6, 7), apply_w=aw)
                apply_done5[g] = r["apply_ve"]

            def emit_iln_apply(st5, gg):
                q1_reads = {}
                tc_gp = {}
                for c in range(DC):
                    i = c % 2
                    w1 = [("ve", st5["s_ve"])] if c == 0 else []
                    if c >= 2:
                        w1.append(("gpc", tc_gp[c - 2]))
                    q1_reads[c] = S.emit(
                        "ve",
                        (lambda cc=c, i=i: lambda e: e.tensor_tensor(
                            ta[i], hs_bf[:, cc * TG:(cc + 1) * TG], ilnr,
                            ALU.mult))(),
                        waits=w1, inc="ve")
                    S.emit("ve",
                           (lambda cc=c, i=i: lambda e: e.tensor_tensor(
                               tb[i], ta[i], ilns, ALU.subtract))(),
                           inc="ve")
                    tb_ve = S.emit(
                        "ve",
                        (lambda cc=c, i=i: lambda e: e.tensor_scalar(
                            tc[i], tb[i], gam_s[:, cc:cc + 1],
                            bet_s[:, cc:cc + 1], ALU.mult, ALU.add))(),
                        inc="ve")
                    tc_gp[c] = S.emit(
                        "gp",
                        (lambda cc=c, i=i, gg=gg: lambda e: e.tensor_tensor(
                            stream_ap(cc, gg * TG, TG), tc[i],
                            stream_ap(cc, gg * TG, TG), ALU.add))(),
                        waits=[("ve", tb_ve)], inc="gpc")
                return q1_reads, tc_gp[DC - 1]

            pend = None
            pend_reads = None
            last_stream_gp = None
            emit_ln5(0)
            for g in range(NG):
                if g + 1 < NG:
                    emit_ln5(g + 1)
                if pend is not None:
                    pend_reads, last_stream_gp = emit_iln_apply(pend, g - 1)
                xn3 = xn3db[g % 2]
                r = {"apply_ve": apply_done5[g]}
                for c in range(DC):
                    b = c % 2
                    w = [p1_ready["sq"][c]]
                    if (g, c) != (0, 0):
                        prev_c = c - 2 if c >= 2 else DC - 2 + c
                        if (prev_c in q_te) and not (g == 0 and c < 2):
                            w.append(("te", q_te[prev_c]))
                    q_dma[c] = S.dma("sy", qw_buf[b][:, :], sq[c, :, :], waits=w)
                    for k in range(DC):
                        w = []
                        if k == 0:
                            w = [q_dma[c], ("ve", r["apply_ve"])]
                            prev_c = c - 2 if c >= 2 else (DC - 2 + c if g > 0 else None)
                            if prev_c is not None and prev_c in q_se3:
                                w.append(("se", q_se3[prev_c]))
                            if c <= 1 and (DC - 2 + c) in hs_ev and g > 0:
                                w.append(("ve", hs_ev[DC - 2 + c]))
                        q_te[c] = S.emit(
                            "te",
                            (lambda kk, cc=c, bb=b, xn3=xn3: lambda e: e.matmul(
                                pb[bb][:, :], qw_buf[bb][:, kk * 128:(kk + 1) * 128],
                                xn3[:, kk * TG:(kk + 1) * TG],
                                start=(kk == 0), stop=(kk == DC - 1)))(k),
                            waits=w, inc="te" if k == DC - 1 else None) or q_te.get(c)
                    se1 = S.emit(
                        "se",
                        (lambda cc=c, bb=b: lambda e: e.activation(
                            t1_bf, pb[bb][:, :], AF.Relu, scale=neg1,
                            bias=neg_bq[:, cc:cc + 1]))(),
                        waits=[("te", q_te[c])], inc="se")
                    se2 = S.emit(
                        "se",
                        lambda e: e.activation(e_bf, t1_bf, AF.Exp, scale=neg1),
                        inc="se")
                    se3 = S.emit(
                        "se",
                        (lambda cc=c, bb=b: lambda e: e.activation(
                            t2_bf, pb[bb][:, :], AF.Relu,
                            bias=bq_s[:, cc:cc + 1]))(),
                        inc="se")
                    q_se3[c] = se3
                    wqv = [("se", se3)]
                    if c == 0 and iln_red_te is not None:
                        wqv.append(("te", iln_red_te))
                    if pend_reads is not None:
                        wqv.append(("ve", pend_reads[c]))
                    q_ve[c] = ("gpc", S.emit(
                        "gp",
                        (lambda cc=c: lambda e: e.tensor_tensor(
                            q1[:, cc * TG:(cc + 1) * TG], t2_bf, e_bf, ALU.add))(),
                        waits=wqv, inc="gpc"))
                q_last_te[g] = q_te[DC - 1]
                ctx_z_se = None
                ctx_z_ve = None
                tp_te_prev2 = None
                for t in range(4):
                    den_mm = None
                    for p in range(NH // 2):
                        bank = pb[2 + p // 4]
                        off = (p % 4) * 128
                        w = []
                        if p == 0:
                            w = [q_ve[DC - 1]]
                            if ctx_z_se is not None:
                                w.append(("se", ctx_z_se))
                        elif p == 4 and ctx_z_ve is not None:
                            w = [("ve", ctx_z_ve)]
                        S.emit(
                            "te",
                            (lambda pp=p, tt=t, bank=bank, off=off: lambda e: e.matmul(
                                bank[:, off:off + 128],
                                q1[:, pp * TG + tt * 128: pp * TG + (tt + 1) * 128],
                                kvks[:, pp * 130: pp * 130 + 128],
                                start=True, stop=True))(),
                            waits=w, inc=None)
                        den_mm = S.emit(
                            "te",
                            (lambda pp=p, tt=t: lambda e: e.matmul(
                                pb[4][:, pp * 2:(pp + 1) * 2],
                                q1[:, pp * TG + tt * 128: pp * TG + (tt + 1) * 128],
                                kvks[:, pp * 130 + 128: pp * 130 + 130],
                                start=True, stop=True))(),
                            inc="te" if p == NH // 2 - 1 else None) or den_mm
                    w = [("te", den_mm)]
                    if ctx_z_se is not None:
                        w.append(("se", ctx_z_se))
                    rec_ve = S.emit(
                        "ve", lambda e: e.reciprocal(z_f, pb[4][:, 0:16]),
                        waits=w, inc="ve")
                    for hh in range(8):
                        bank = pb[2]
                        off = hh * 64
                        w = [("ve", rec_ve)] if hh == 0 else []
                        if hh == 0 and tp_te_prev2 is not None:
                            w.append(("te", tp_te_prev2))
                        ctx_z_se = S.emit(
                            "se",
                            (lambda hh=hh, tt=t, bank=bank, off=off: lambda e:
                             e.activation(
                                 ctx_bf[tt % 2][:, hh * 64:(hh + 1) * 64],
                                 bank[:, off:off + 64], AF.Identity,
                                 scale=z_f[:, hh:hh + 1]))(),
                            waits=w, inc="se")
                    for hh in range(8, NH):
                        bank = pb[3]
                        off = (hh - 8) * 64
                        w = []
                        if hh == 8 and tp_te_prev2 is not None:
                            w.append(("te", tp_te_prev2))
                        ctx_z_ve = S.emit(
                            "ve",
                            (lambda hh=hh, tt=t, bank=bank, off=off: lambda e:
                             e.tensor_scalar(
                                 ctx_bf[tt % 2][:, hh * 64:(hh + 1) * 64],
                                 bank[:, off:off + 64], z_f[:, hh:hh + 1],
                                 None, ALU.mult))(),
                            waits=w, inc="ve")
                    for rnd in range(2):
                        w = ([("se", ctx_z_se)] if rnd == 0
                             else [("ve", ctx_z_ve)])
                        if tp_ve_prev is not None:
                            w.append(("ve", tp_ve_prev))

                        def mkt(tt=t, rr=rnd):
                            def f(e):
                                ins = None
                                for j in range(4):
                                    m = rr * 4 + j
                                    ins = e.matmul(
                                        pb[5][:, j * 128:(j + 1) * 128],
                                        ctx_bf[tt % 2][:, m * 128:(m + 1) * 128],
                                        ident_b[:], start=True, stop=True)
                                return ins
                            return f
                        tp_te = S.emit("te", mkt(), waits=w, inc="te")
                        tp_te_prev2 = tp_te

                        def mke(tt=t, rr=rnd):
                            def f(e):
                                dst = ctx_fm.rearrange(
                                    "p (m l) -> p m l", m=DC)[:, rr * 4:(rr + 1) * 4,
                                                             tt * 128:(tt + 1) * 128]
                                src = pb[5][:, :].rearrange("p (m l) -> p m l", m=4)
                                return e.tensor_copy(dst, src)
                            return f
                        tp_ve_prev = S.emit("ve", mke(), waits=[("te", tp_te)],
                                            inc="ve")
                for cd in range(DC):
                    b = cd % 2
                    w = [p1_ready["sdd"][cd]]
                    if cd >= 2:
                        w.append(("te", hs_te[cd - 2]))
                    elif g > 0:
                        w.append(("te", hs_te[DC - 2 + cd]))
                    c_dma = S.dma("sy", dw_buf[b][:, :], sdd[cd, :, :], waits=w)
                    for m in range(DC):
                        w = []
                        if m == 0:
                            w = [c_dma, ("ve", tp_ve_prev)]
                            if cd >= 2:
                                w.append(("ve", hs_ev[cd - 2]))
                            else:
                                w.append(("se", q_se3[DC - 2 + cd]))
                        hs_te[cd] = S.emit(
                            "te",
                            (lambda mm, cc=cd, bb=b: lambda e: e.matmul(
                                pb[bb][:, :], dw_buf[bb][:, mm * 128:(mm + 1) * 128],
                                ctx_fm[:, mm * TG:(mm + 1) * TG],
                                start=(mm == 0), stop=(mm == DC - 1)))(m),
                            waits=w, inc="te" if m == DC - 1 else None) or hs_te.get(cd)
                    wev = [("te", hs_te[cd])]
                    if pend is not None:
                        pass
                    hs_ev[cd] = S.emit(
                        "ve",
                        (lambda cc=cd, bb=b, xn3=xn3: lambda e: e.scalar_tensor_tensor(
                            hs_bf[:, cc * TG:(cc + 1) * TG], pb[bb][:, :],
                            bd_s[:, cc:cc + 1], xn3[:, cc * TG:(cc + 1) * TG],
                            ALU.add, ALU.add))(),
                        waits=wev, inc="ve")
                prev = sc.get("prev", {})
                x2_te_prev2 = prev.get("x2_te")
                sq_prev = None
                te_c = None
                x2m = {}
                x2_te_prev3 = prev.get("x2_te")
                for c in range(DC):
                    x2s = sc["x2"]
                    w_ve = [("ve", hs_ev[c])]
                    if x2_te_prev3 is not None:
                        w_ve.append(("te", x2_te_prev3))
                    sq_prev = S.emit(
                        "ve",
                        (lambda cc=c, x2s=x2s: lambda e: e.tensor_tensor(
                            x2s, hs_bf[:, cc * TG:(cc + 1) * TG],
                            hs_bf[:, cc * TG:(cc + 1) * TG], ALU.mult))(),
                        waits=w_ve, inc="ve")
                    w = [("ve", hs_ev[c])]
                    if c == 0 and "stats_ve" in prev:
                        w.append(("ve", prev["stats_ve"]))
                    S.emit(
                        "te",
                        (lambda cc=c: lambda e: e.matmul(
                            pb[6][:, :], ones_b[:], hs_bf[:, cc * TG:(cc + 1) * TG],
                            start=(cc == 0), stop=(cc == DC - 1)))(c),
                        waits=w, inc="te")
                    te_c = S.emit(
                        "te",
                        (lambda cc=c, x2s=x2s: lambda e: e.matmul(
                            pb[7][:, :], ones_b[:], x2s,
                            start=(cc == 0), stop=(cc == DC - 1)))(c),
                        waits=[("ve", sq_prev)], inc="te")
                    x2_te_prev3 = te_c
                x2_te_prev2 = te_c
                iln_red_te = te_c
                ve_c = S.emit("ve", lambda e: e.tensor_scalar(
                    sc["mu"], pb[6][:, 0:TG], 1.0 / D, None, ALU.mult),
                    waits=[("te", iln_red_te)], inc="ve")
                ve_c = S.emit("ve", lambda e: e.tensor_tensor(
                    sc["s2"], sc["mu"], sc["mu"], ALU.mult), inc="ve")
                stats_ve = S.emit("ve", lambda e: e.scalar_tensor_tensor(
                    iln_va, pb[7][:, 0:TG], 1.0 / D, sc["s2"], ALU.mult,
                    ALU.subtract),
                    waits=[("te", iln_red_te)], inc="ve")
                se_c = S.emit("se", lambda e: e.activation(
                    iln_va, iln_va, AF.Ln, bias=cvals[:, 1:2]),
                    waits=[("ve", stats_ve)], inc="se")
                r_se = S.emit("se", lambda e: e.activation(
                    ilnr, iln_va, AF.Exp, scale=cvals[:, 2:3]), inc="se")
                s_ve = S.emit("ve", lambda e: e.tensor_tensor(
                    ilns, sc["mu"], ilnr, ALU.mult),
                    waits=[("se", r_se)], inc="ve")
                sc["prev"] = {"stats_ve": stats_ve, "red_te": iln_red_te,
                              "x2_te": x2_te_prev2}
                pend = {"s_ve": s_ve, "r_se": r_se}
            pend_reads, last_stream_gp = emit_iln_apply(pend, NG - 1)
            f32p.pop(); bf16p.pop()
            return S.val("ve")

        if upto >= 4:
            kvks = bf16p.t[:, KVKS_OFF:KVKS_OFF + NH // 2 * 130]
            p4_snap = p4_phase(kvks)
            if upto >= 5:
                last_ve = p5_phase(kvks, p4_snap)

        if upto >= 6:
            last_ve = ffn_phase(S.snap())

        f32p.push()
        S.fence(("te", "ve"), S.snap())
        ot = [f32p.alloc(D) for _ in range(2)]
        out_dma = {}
        pve = {}
        for t in range(NT):
            b = t % 2
            for half in range(2):
                bank = pb[2 * b + half]
                w = []
                if t >= 2 and half == 0:
                    w.append(out_dma[t - 2])

                def mk(tt=t, hh=half, bank=bank):
                    def f(e):
                        ins = None
                        src = stream_chunks_ap(tt * 128, 128, c0=hh * 4, cn=4)
                        for j in range(4):
                            ins = e.transpose(bank[:, j * 128:(j + 1) * 128],
                                              src[:, j, :], ident_f[:])
                        return ins
                    return f
                c_te = S.emit("te", mk(), waits=w, inc="te")
                pve[(t, half)] = S.emit(
                    "ve",
                    (lambda tt=t, bb=b, hh=half, bank=bank: lambda e:
                     e.tensor_copy(ot[bb][:, hh * 512:(hh + 1) * 512],
                                   bank[:, :]))(),
                    waits=[("te", c_te)], inc="ve")
            out_dma[t] = S.dma(
                "sy", y_out[t * 128:(t + 1) * 128, :], ot[b][:, :],
                waits=[("ve", pve[(t, 1)])])
        f32p.pop()
        S.wait("sy", [(f"sy{i}", S.val(f"sy{i}")) for i in range(DMA_R)]
               + [(f"gp{i}", S.val(f"gp{i}")) for i in range(DMA_R)])

        with nc.Block() as block:
            @block.sync
            def _(eng):
                for t in S.q["sy"]:
                    t(eng)

            @block.gpsimd
            def _(eng):
                for t in S.q["gp"]:
                    t(eng)

            @block.tensor
            def _(eng):
                for t in S.q["te"]:
                    t(eng)

            @block.vector
            def _(eng):
                for t in S.q["ve"]:
                    t(eng)

            @block.scalar
            def _(eng):
                for t in S.q["se"]:
                    t(eng)

    return nc


_PROG_CACHE = {}


def kernel(**inputs):
    upto = int(inputs.pop("_upto", 99))
    x = np.asarray(inputs["L0_attn_a_x"], np.float32)
    d = host_prep(inputs)
    if upto not in _PROG_CACHE:
        _PROG_CACHE[upto] = build_program(upto)
    nc = _PROG_CACHE[upto]
    shared = {
        "wgT": d["wgT"], "wuT": d["wuT"], "wdT": d["wdT"],
        "wqT": d["wqT"], "wkT": d["wkT"], "wvT": d["wvT"], "wddT": d["wddT"],
        "bwT": d["bwT"], "cwT": d["cwT"], "dD": d["dD"], "logA": d["logA"],
        "bq_p": d["bq_p"], "bd_p": d["bd_p"], "gamma_p": d["gamma_p"],
        "beta_p": d["beta_p"], "bk_r": d["bk_r"], "bv_r": d["bv_r"],
        "ident_in": d["ident"],
    }
    in_maps = []
    for c in range(8):
        m = dict(shared)
        m["x_in"] = np.ascontiguousarray(x[c])
        in_maps.append(m)
    res = run_bass_kernel_spmd(nc, in_maps, core_ids=list(range(8)))
    out = np.stack([res.results[c]["y_out"] for c in range(8)], axis=0)
    return out
